# version 10
# baseline (speedup 1.0000x reference)
import numpy as np
from contextlib import ExitStack
import concourse.bass as bass
import concourse.mybir as mybir
from concourse.bass_utils import run_bass_kernel_spmd

F32 = mybir.dt.float32
BF16 = mybir.dt.bfloat16
AF = mybir.ActivationFunctionType
ALU = mybir.AluOpType
AX = mybir.AxisListType

B, T, D, H, HD, INTER = 2, 4096, 2048, 16, 128, 5632
NCORES, GRP = 8, 4
NT = 1024
CC = D // 128
ICH = INTER // 128
IGRP = 2
ICG = ICH // IGRP
HPC = 4
NG = 8
QT = 4
SC = float(1.0 / np.sqrt(128.0))

RG = [[0, 1, 2, 3], [4, 5, 6, 7]]

COL_GS = 0
COL_US = 44
COL_DS = 88
COL_BO = 104
COL_BQ = 120
COL_BK = 124
COL_BV = 128
NCOLS = 132


def build_nc(phases=("ffn1", "attn", "ffn2")):
    nc = bass.Bass("TRN2", target_bir_lowering=False, num_devices=8)

    xT = nc.dram_tensor("xT", [D, NT], F32, kind="ExternalInput").ap()
    gwT = nc.dram_tensor("gwT", [D, INTER], F32, kind="ExternalInput").ap()
    uwT = nc.dram_tensor("uwT", [D, INTER], F32, kind="ExternalInput").ap()
    dwT = nc.dram_tensor("dwT", [INTER, D], F32, kind="ExternalInput").ap()
    wq = nc.dram_tensor("wq", [D, HPC * HD], F32, kind="ExternalInput").ap()
    wk = nc.dram_tensor("wk", [D, HPC * HD], F32, kind="ExternalInput").ap()
    wv = nc.dram_tensor("wv", [D, HPC * HD], F32, kind="ExternalInput").ap()
    wo = nc.dram_tensor("wo", [D, D], F32, kind="ExternalInput").ap()
    colsD = nc.dram_tensor("cols", [128, NCOLS], F32, kind="ExternalInput").ap()
    triD = nc.dram_tensor("trimask", [128, 128], F32, kind="ExternalInput").ap()
    identD = nc.dram_tensor("identf", [128, 128], F32, kind="ExternalInput").ap()
    yout = nc.dram_tensor("yout", [D, NT], F32, kind="ExternalOutput").ap()

    h2src = nc.dram_tensor("h2src", [D, NT], BF16, kind="Internal").ap()
    h2all = nc.dram_tensor("h2all", [GRP * D, NT], BF16, kind="Internal").ap()
    qD = nc.dram_tensor("qD", [HPC * HD, T], BF16, kind="Internal").ap()
    kD = nc.dram_tensor("kD", [HPC * HD, T], BF16, kind="Internal").ap()
    vD = nc.dram_tensor("vD", [HPC * T, HD], BF16, kind="Internal").ap()
    ysrc = nc.dram_tensor("ysrc", [HPC * HD, T], BF16, kind="Internal").ap()
    yallD = nc.dram_tensor("yallD", [D, T], BF16, kind="Internal").ap()
    dbg = {}
    dbg2 = {}
    if "dbg2" in phases:
        for nm, shp, dt in (("dP", [128, 16384], BF16), ("dnm", [128, 32], F32),
                            ("dsm", [128, 32], F32), ("dgr", [128, 32], F32),
                            ("dsc", [128, 4], F32), ("dysb", [128, 4096], BF16),
                            ("dpt", [128, 4096], BF16)):
            dbg2[nm] = nc.dram_tensor(nm, shp, dt, kind="ExternalOutput").ap()
    if "dbg" in phases:
        for nm, src in (("dq", qD), ("dk", kD), ("dv", vD), ("dy", ysrc), ("dyall", yallD), ("dh2", h2all)):
            dbg[nm] = (nc.dram_tensor(nm, list(src.shape), BF16, kind="ExternalOutput").ap(), src)

    gwT_r = gwT.rearrange("(cc p) i -> p cc i", p=128)
    uwT_r = uwT.rearrange("(cc p) i -> p cc i", p=128)
    dwT_r = dwT.rearrange("(ic p) h -> p ic h", p=128)
    wq_r = wq.rearrange("(cc p) d -> p cc d", p=128)
    wk_r = wk.rearrange("(cc p) d -> p cc d", p=128)
    wv_r = wv.rearrange("(cc p) d -> p cc d", p=128)
    wo_r = wo.rearrange("(hc p) c -> p hc c", p=128)
    yout_r = yout.rearrange("(cc p) t -> p cc t", p=128)
    h2src_r = h2src.rearrange("(cc p) t -> p cc t", p=128)
    h2all_r = h2all.rearrange("(r cc p) t -> p r cc t", p=128, cc=CC)
    qD_r = qD.rearrange("(h p) t -> p h t", p=128)
    kD_r = kD.rearrange("(h p) t -> p h t", p=128)
    vD_r = vD.rearrange("(h tb p) d -> p h tb d", p=128, tb=T // 128)
    ysrc_r = ysrc.rearrange("(h p) t -> p h t", p=128)
    yallD_r = yallD.rearrange("(hc p) t -> p hc t", p=128)

    with ExitStack() as _ctx:
        A1 = _ctx.enter_context(nc.sbuf_tensor("A1", [128, 16384], BF16))
        A2 = _ctx.enter_context(nc.sbuf_tensor("A2", [128, 24576], BF16))
        A3 = _ctx.enter_context(nc.sbuf_tensor("A3", [128, 16384], BF16))
        A4 = _ctx.enter_context(nc.sbuf_tensor("A4", [128, 24576], BF16))
        sqb = _ctx.enter_context(nc.sbuf_tensor("sqb", [128, 2048], BF16))
        tta = _ctx.enter_context(nc.sbuf_tensor("tta", [128, 2048], BF16))
        tvf = _ctx.enter_context(nc.sbuf_tensor("tvf", [128, 1024], BF16))
        ptsb = _ctx.enter_context(nc.sbuf_tensor("ptsb", [128, 4096], BF16))
        tf32 = _ctx.enter_context(nc.sbuf_tensor("tf32", [128, 2048], F32))
        colsf = _ctx.enter_context(nc.sbuf_tensor("colsf", [128, NCOLS], F32))
        trif = _ctx.enter_context(nc.sbuf_tensor("trif", [128, 128], F32))
        identb = _ctx.enter_context(nc.sbuf_tensor("identb", [128, 128], BF16))
        trib = _ctx.enter_context(nc.sbuf_tensor("trib", [128, 128], BF16))
        onesb = _ctx.enter_context(nc.sbuf_tensor("onesb", [128, 1], BF16))
        onesrow = _ctx.enter_context(nc.sbuf_tensor("onesrow", [1, 128], F32))
        epsc = _ctx.enter_context(nc.sbuf_tensor("epsc", [128, 1], F32))
        murow = _ctx.enter_context(nc.sbuf_tensor("murow", [1, 512], F32))
        varrow = _ctx.enter_context(nc.sbuf_tensor("varrow", [1, 512], F32))
        rrow = _ctx.enter_context(nc.sbuf_tensor("rrow", [1, 512], F32))
        murrow = _ctx.enter_context(nc.sbuf_tensor("murrow", [1, 512], F32))
        s1row = _ctx.enter_context(nc.sbuf_tensor("s1row", [1, 512], F32))
        nmrow = _ctx.enter_context(nc.sbuf_tensor("nmrow", [128, 64], F32))
        smrow = _ctx.enter_context(nc.sbuf_tensor("smrow", [128, 64], F32))
        frow = _ctx.enter_context(nc.sbuf_tensor("frow", [128, 64], F32))
        growc = _ctx.enter_context(nc.sbuf_tensor("growc", [128, 64], F32))
        nminc = _ctx.enter_context(nc.sbuf_tensor("nminc", [128, 8], F32))
        scol = _ctx.enter_context(nc.sbuf_tensor("scol", [128, 8], F32))
        pb0 = _ctx.enter_context(nc.psum_tensor("pb0", [128, 512], F32))
        pb1 = _ctx.enter_context(nc.psum_tensor("pb1", [128, 512], F32))
        pb2 = _ctx.enter_context(nc.psum_tensor("pb2", [128, 512], F32))
        pbv = _ctx.enter_context(nc.psum_tensor("pbv", [128, 512], F32))
        ptrB = _ctx.enter_context(nc.psum_tensor("ptrB", [128, 2048], BF16))
        ptrA = _ctx.enter_context(nc.psum_tensor("ptrA", [128, 2048], BF16))
        sPE = _ctx.enter_context(nc.semaphore("sPE"))
        sDV = _ctx.enter_context(nc.semaphore("sDV"))
        sAC = _ctx.enter_context(nc.semaphore("sAC"))
        sGP = _ctx.enter_context(nc.semaphore("sGP"))
        block = _ctx.enter_context(nc.Block())
        pb = [pb0, pb1, pb2, pbv]

        sched = {k: [] for k in ("tensor", "vector", "scalar", "gpsimd", "sync")}
        cnt = {"sPE": 0, "sDV": 0, "sAC": 0, "sGP": 0}
        sem = {"sPE": sPE, "sDV": sDV, "sAC": sAC, "sGP": sGP}
        maxwait = dict(cnt)

        def S(name):
            if name not in sem:
                sem[name] = _ctx.enter_context(nc.semaphore(name))
                cnt[name] = 0
                maxwait[name] = 0
            return sem[name]

        def E(eng, make, wait=(), inc=None):
            wl = []
            for sname, v in wait:
                if v > 0:
                    S(sname)
                    maxwait[sname] = max(maxwait[sname], v)
                    wl.append((sem[sname], v))
            ib = None
            if inc:
                sname, amt = inc
                S(sname)
                cnt[sname] += amt
                ib = (sem[sname], amt)

            def run(e, make=make, wl=wl, ib=ib):
                for s, v in wl:
                    e.wait_ge(s, v)
                ins = make(e)
                if ib is not None:
                    ins.then_inc(*ib)
            sched[eng].append(run)
            return cnt[inc[0]] if inc else None

        pend = {}

        def bank_wait(bid):
            return pend.pop(bid, [])

        def bank_read(bid, sname, val):
            pend.setdefault(bid, []).append((sname, val))

        import dataclasses as _dc

        def bc0(ap, n):
            return _dc.replace(ap, ap=[list(p) for p in ap.ap] + [[0, n]])

        ENGSEM = {"tensor": "sPE", "vector": "sDV", "scalar": "sAC", "gpsimd": "sGP"}

        def ES(eng, make, wait=()):
            sname = ENGSEM[eng]
            v = E(eng, make, wait=wait, inc=(sname, 1))
            maxwait[sname] = max(maxwait[sname], v)

            def w(e, ss=sem[sname], v=v):
                e.wait_ge(ss, v)
            sched[eng].append(w)
            return v

        E("sync", lambda e: e.dma_start(out=colsf[:], in_=colsD), inc=("iniS", 16))
        E("sync", lambda e: e.dma_start(out=trif[:], in_=triD), inc=("iniS", 16))
        c_init = E("sync", lambda e: e.dma_start(out=yout, in_=xT), inc=("xinit", 16))
        E("gpsimd", lambda e: e.dma_start(out=identb[:], in_=identD), inc=("iniW", 16))
        E("gpsimd", lambda e: e.dma_start(out=trib[:], in_=triD), inc=("iniW", 16))
        E("gpsimd", lambda e: e.memset(onesb[:], 1.0), inc=("sGP", 1))
        E("gpsimd", lambda e: e.memset(onesrow[:], 1.0), inc=("sGP", 1))
        c_memset = E("gpsimd", lambda e: e.memset(epsc[:], 1e-5), inc=("sGP", 1))
        E("vector", lambda e: e.tensor_scalar_mul(colsf[:, COL_BQ:COL_BQ + 4],
                                                  colsf[:, COL_BQ:COL_BQ + 4], SC),
          wait=[("iniS", 32)])
        c_cols = E("vector", lambda e: e.tensor_scalar_mul(colsf[:, COL_DS:COL_DS + 16],
                                                           colsf[:, COL_DS:COL_DS + 16], 0.5),
                   inc=("sDV", 1))

        st = {
            "ln_src_wait": [("xinit", c_init), ("sDV", c_cols)],
            "xn_ready": 0,
            "a2_free": [],
            "a4_free": [("iniS", 32), ("iniW", 32)],
            "p_free": 0,
        }
        tvf_free = {}
        ptsb_free = {}
        tf_free = {}

        def emit_ln(src_r=None, chunk_wait=None):
            xb = A1
            src_r_ = yout_r if src_r is None else src_r
            cWs = {}
            stat_pe = {}

            def emit_cast(ccx):
                w = list(st["ln_src_wait"]) if chunk_wait is None else chunk_wait(ccx)
                if ccx >= 4:
                    w = w + [("sPE", stat_pe[ccx - 4])]
                cWs[ccx] = ("lnc%d" % (ccx % 4), E("gpsimd", lambda e, ccx=ccx: e.dma_start(
                    out=xb[:, ccx * 1024:(ccx + 1) * 1024],
                    in_=src_r_[:, ccx, :]), wait=w, inc=("lnc%d" % (ccx % 4), 16)))

            for ccx in range(4):
                emit_cast(ccx)
            sq_done = [0, 0]
            mm_done = [0, 0]
            for ccx in range(CC):
                if ccx >= 1 and ccx + 3 < CC and (ccx + 3) >= 4:
                    emit_cast(ccx + 3)
                par = ccx % 2
                w = [cWs[ccx], ("sGP", c_memset)]
                if ccx >= 2:
                    w.append(("sPE", mm_done[par]))
                sq_done[par] = E("scalar", lambda e, ccx=ccx, par=par: e.activation(
                    out=sqb[:, par * 1024:(par + 1) * 1024],
                    in_=xb[:, ccx * 1024:(ccx + 1) * 1024], func=AF.Square),
                    wait=w, inc=("sAC", 1))
                for th in range(2):
                    w = [cWs[ccx]]
                    if ccx == 0:
                        w += [("sGP", c_memset)] + bank_wait(("pb", th))
                    E("tensor", lambda e, ccx=ccx, th=th: e.matmul(
                        pb[th][0:1, :], onesb[:],
                        xb[:, ccx * 1024 + th * 512: ccx * 1024 + th * 512 + 512],
                        start=(ccx == 0), stop=(ccx == CC - 1)), wait=w)
                for th in range(2):
                    w = [("sAC", sq_done[par])]
                    if ccx == 0:
                        w += bank_wait(("pb", 2 + th))
                    mm_done[par] = E("tensor", lambda e, ccx=ccx, th=th, par=par: e.matmul(
                        pb[2 + th][0:1, :], onesb[:],
                        sqb[:, par * 1024 + th * 512: par * 1024 + th * 512 + 512],
                        start=(ccx == 0), stop=(ccx == CC - 1)),
                        wait=w, inc=("sPE", 1))
                stat_pe[ccx] = mm_done[par]
            st["ln_stat_pe"] = stat_pe
            c_stats = mm_done[(CC - 1) % 2]
            cn = 0
            for th in range(2):
                ES("vector", lambda e, th=th: e.tensor_scalar_mul(
                    murow[0:1, :], pb[th][0:1, :], 1.0 / D),
                    wait=[("sPE", c_stats)] + ([("sPE", cnt["sPE"])] if th else []))
                ES("vector", lambda e: e.tensor_mul(
                    s1row[0:1, :], murow[0:1, :], murow[0:1, :]))
                cv = ES("vector", lambda e, th=th: e.scalar_tensor_tensor(
                    varrow[0:1, :], pb[2 + th][0:1, :], 1.0 / D, s1row[0:1, :],
                    ALU.mult, ALU.subtract))
                ES("scalar", lambda e: e.activation(
                    out=rrow[0:1, :], in_=varrow[0:1, :], func=AF.Ln, bias=epsc[0:1, :]),
                    wait=[("sDV", cv)])
                ca = ES("scalar", lambda e: e.activation(
                    out=rrow[0:1, :], in_=rrow[0:1, :], func=AF.Exp, scale=-0.5))
                cmu = ES("vector", lambda e: e.tensor_mul(
                    murrow[0:1, :], murow[0:1, :], rrow[0:1, :]),
                    wait=[("sAC", ca)])
                E("tensor", lambda e, th=th: e.matmul(
                    pb[th][:], onesrow[:], rrow[0:1, :], start=True, stop=True),
                    wait=[("sDV", cmu)])
                cbc = E("tensor", lambda e, th=th: e.matmul(
                    pb[2 + th][:], onesrow[:], murrow[0:1, :], start=True, stop=True),
                    inc=("sPE", 1))
                for ccx in range(CC):
                    s0 = ccx * 1024 + th * 512
                    ES("vector", lambda e, s0=s0, th=th: e.tensor_tensor(
                        xb[:, s0:s0 + 512], xb[:, s0:s0 + 512], pb[th][:], ALU.mult),
                        wait=[("sPE", cbc)] if ccx == 0 else [])
                    cn = E("vector", lambda e, s0=s0, th=th: e.tensor_tensor(
                        xb[:, s0:s0 + 512], xb[:, s0:s0 + 512], pb[2 + th][:], ALU.subtract),
                        inc=("sDV", 1))
                bank_read(("pb", th), "sDV", cn)
                bank_read(("pb", 2 + th), "sDV", cn)
            st["xn_ready"] = cn

        def emit_ffn():
            xb = A1
            hdn = A2
            gsl = [A4[:, 0:4096], A4[:, 4096:8192]]
            usl = [A4[:, 8192:12288], A4[:, 12288:16384]]
            dsl = [A4[:, 16384:16384 + 2816], A4[:, 16384 + 2816:16384 + 5632]]
            rot = [0]
            ttx = [0]
            last_hdn = 0
            for grp in range(IGRP):
                slab = {}
                for j in range(11):
                    if j == 0:
                        c0 = grp * (ICG * 128)
                        w = []
                        if grp == 0:
                            w += st["a4_free"]
                        else:
                            w.append(("sPE", cnt["sPE"]))
                        slab[(0, "g")] = ("gsA", E("gpsimd", lambda e, c0=c0: e.dma_start(
                            out=gsl[0].rearrange("p (cc i) -> p cc i", i=256),
                            in_=gwT_r[:, :, c0:c0 + 256]), wait=w, inc=("gsA", 16)))
                        slab[(0, "u")] = ("usA", E("gpsimd", lambda e, c0=c0: e.dma_start(
                            out=usl[0].rearrange("p (cc i) -> p cc i", i=256),
                            in_=uwT_r[:, :, c0:c0 + 256]), inc=("usA", 16)))
                    if j + 1 < 11:
                        c1 = grp * (ICG * 128) + (j + 1) * 256
                        b1 = (j + 1) % 2
                        w = [("sPE", slab.get((j - 1, "pe"), 0))]
                        gf = "gsA" if b1 == 0 else "gsB"
                        uf = "usA" if b1 == 0 else "usB"
                        slab[(j + 1, "g")] = (gf, E("gpsimd", lambda e, c1=c1, b1=b1: e.dma_start(
                            out=gsl[b1].rearrange("p (cc i) -> p cc i", i=256),
                            in_=gwT_r[:, :, c1:c1 + 256]), wait=w, inc=(gf, 16)))
                        slab[(j + 1, "u")] = (uf, E("gpsimd", lambda e, c1=c1, b1=b1: e.dma_start(
                            out=usl[b1].rearrange("p (cc i) -> p cc i", i=256),
                            in_=uwT_r[:, :, c1:c1 + 256]), inc=(uf, 16)))
                    bsl = j % 2
                    for icl in range(2):
                        icg = j * 2 + icl
                        ica = grp * ICG + icg
                        for th in range(2):
                            bg = rot[0] % 4
                            bu = (rot[0] + 1) % 4
                            rot[0] += 2
                            wpe = [slab[(j, "g")], slab[(j, "u")], ("sDV", st["xn_ready"])]
                            wpe += bank_wait(("pb", bg)) + bank_wait(("pb", bu))
                            for k in range(CC):
                                E("tensor", lambda e, k=k, icl=icl, th=th, bsl=bsl, bg=bg: e.matmul(
                                    pb[bg][:],
                                    gsl[bsl][:, k * 256 + icl * 128: k * 256 + icl * 128 + 128],
                                    xb[:, k * 1024 + th * 512: k * 1024 + th * 512 + 512],
                                    start=(k == 0), stop=(k == CC - 1)),
                                    wait=(wpe if k == 0 else []))
                            cpe = 0
                            for k in range(CC):
                                cpe = E("tensor", lambda e, k=k, icl=icl, th=th, bsl=bsl, bu=bu: e.matmul(
                                    pb[bu][:],
                                    usl[bsl][:, k * 256 + icl * 128: k * 256 + icl * 128 + 128],
                                    xb[:, k * 1024 + th * 512: k * 1024 + th * 512 + 512],
                                    start=(k == 0), stop=(k == CC - 1)),
                                    inc=("sPE", 1) if k == CC - 1 else None) or cpe
                            slab[(j, "pe")] = cpe
                            st["xb_free_pe"] = cpe
                            st["gu_done_pe"] = cpe
                            t0 = (ttx[0] % 2) * 1024
                            ttx[0] += 1
                            wtt = tvf_free.pop(("tta", t0), [])
                            ca1 = E("scalar", lambda e, bg=bg, ica=ica, t0=t0: e.activation(
                                out=tta[:, t0:t0 + 512], in_=pb[bg][:], func=AF.Silu,
                                scale=colsf[:, COL_GS + ica:COL_GS + ica + 1]),
                                wait=[("sPE", cpe)] + wtt, inc=("sAC", 1))
                            ca2 = E("scalar", lambda e, bu=bu, ica=ica, t0=t0: e.activation(
                                out=tta[:, t0 + 512:t0 + 1024], in_=pb[bu][:], func=AF.Identity,
                                scale=colsf[:, COL_US + ica:COL_US + ica + 1]),
                                inc=("sAC", 1))
                            bank_read(("pb", bg), "sAC", ca1)
                            bank_read(("pb", bu), "sAC", ca2)
                            hs = icg * 1024 + th * 512
                            last_hdn = E("vector", lambda e, hs=hs, t0=t0: e.tensor_mul(
                                hdn[:, hs:hs + 512], tta[:, t0:t0 + 512], tta[:, t0 + 512:t0 + 1024]),
                                wait=[("sAC", ca2)], inc=("sDV", 1))
                            tvf_free[("tta", t0)] = [("sDV", last_hdn)]
                dd = {}
                for hc in range(16):
                    if hc == 0:
                        w = [("sPE", cnt["sPE"])] if grp > 0 else []
                        dd[0] = ("dsA", E("gpsimd", lambda e, grp=grp: e.dma_start(
                            out=dsl[0].rearrange("p (ic h) -> p ic h", h=128),
                            in_=dwT_r[:, grp * ICG:(grp + 1) * ICG, 0:128]),
                            wait=w, inc=("dsA", 16)))
                    if hc + 1 < 16:
                        b1 = (hc + 1) % 2
                        df = "dsA" if b1 == 0 else "dsB"
                        dd[hc + 1] = (df, E("gpsimd", lambda e, grp=grp, hc=hc, b1=b1: e.dma_start(
                            out=dsl[b1].rearrange("p (ic h) -> p ic h", h=128),
                            in_=dwT_r[:, grp * ICG:(grp + 1) * ICG, (hc + 1) * 128:(hc + 2) * 128]),
                            wait=[("sPE", dd.get(("pe", hc - 1), 0))], inc=(df, 16)))
                    bsl = hc % 2
                    for th in range(2):
                        bk = rot[0] % 4
                        rot[0] += 1
                        wpe = [dd[hc], ("sDV", last_hdn)] + bank_wait(("pb", bk))
                        cpe = 0
                        for k in range(ICG):
                            cpe = E("tensor", lambda e, k=k, th=th, bsl=bsl, bk=bk: e.matmul(
                                pb[bk][:],
                                dsl[bsl][:, k * 128:k * 128 + 128],
                                hdn[:, k * 1024 + th * 512: k * 1024 + th * 512 + 512],
                                start=(k == 0), stop=(k == ICG - 1)),
                                wait=(wpe if k == 0 else []),
                                inc=("sPE", 1) if k == ICG - 1 else None) or cpe
                        dd[("pe", hc)] = cpe
                        fs = (rot[0] % 4) * 512
                        wf = tf_free.pop(fs, [])
                        cev = E("scalar", lambda e, bk=bk, hc=hc, fs=fs: e.activation(
                            out=tf32[:, fs:fs + 512], in_=pb[bk][:], func=AF.Identity,
                            scale=colsf[:, COL_DS + hc:COL_DS + hc + 1]),
                            wait=[("sPE", cpe)] + wf, inc=("sAC", 1))
                        bank_read(("pb", bk), "sAC", cev)
                        af = "acc%d" % (fs // 512)
                        cdma = E("gpsimd", lambda e, hc=hc, th=th, fs=fs: e.dma_start(
                            out=yout_r[:, hc, th * 512:(th + 1) * 512],
                            in_=tf32[:, fs:fs + 512], accum_op=ALU.add),
                            wait=[("sAC", cev), ("xinit", c_init)], inc=(af, 16))
                        tf_free[fs] = [(af, cdma)]
                        st.setdefault("chunk_done", {}).setdefault(hc, []).append((af, cdma))
            st["ln_src_wait"] = [("acc%d" % i, cnt.get("acc%d" % i, 0)) for i in range(4)] + \
                [("sPE", cnt["sPE"])]
            st["a2_free"] = [("sPE", cnt["sPE"])]
            st["a4_free"] = [("sPE", cnt["sPE"])]

        def emit_qkv():
            xb = A1
            cS = E("sync", lambda e: e.dma_start(
                out=h2src_r, in_=xb[:].rearrange("p (cc t) -> p cc t", cc=CC)),
                wait=[("sDV", st["xn_ready"])], inc=("h2s", 16))
            ccc = E("gpsimd", lambda e: e.collective_compute(
                "AllGather", ALU.bypass, RG, ins=[h2src], outs=[h2all]),
                wait=[("h2s", cS)], inc=("cc1", 1))
            wqb = A4[:, 0:8192]
            wkb = A4[:, 8192:16384]
            wvb = A4[:, 16384:24576]
            E("gpsimd", lambda e: e.dma_start(
                out=wqb.rearrange("p (cc d) -> p cc d", d=512), in_=wq_r),
                wait=[("sPE", st.get("gu_done_pe", 0))], inc=("wqv", 16))
            E("gpsimd", lambda e: e.dma_start(
                out=wkb.rearrange("p (cc d) -> p cc d", d=512), in_=wk_r), inc=("wqv", 16))
            cw = E("gpsimd", lambda e: e.dma_start(
                out=wvb.rearrange("p (cc d) -> p cc d", d=512), in_=wv_r),
                wait=st["a4_free"], inc=("wqv", 16))
            h2sl = [A3[:, 0:8192], A3[:, 8192:16384]]
            slab = {}
            rot = [0]
            tr = [0]
            for s in range(8):
                bsl = s % 2
                wsl = [("cc1", ccc)] if s < 2 else [("sPE", slab[("pe", s - 2)])]
                hf = "h2A" if bsl == 0 else "h2B"
                slab[s] = (hf, E("sync", lambda e, s=s, bsl=bsl: e.dma_start(
                    out=h2sl[bsl].rearrange("p (cc t) -> p cc t", t=512),
                    in_=h2all_r[:, s // 2, :, (s % 2) * 512:(s % 2) * 512 + 512]),
                    wait=wsl, inc=(hf, 16)))
                for hh in range(HPC):
                    for which in ("q", "k", "v"):
                        wslb = {"q": wqb, "k": wkb, "v": wvb}[which]
                        bkk = rot[0] % 4
                        rot[0] += 1
                        wpe = [slab[s], ("wqv", cw)] + bank_wait(("pb", bkk))
                        cpe = 0
                        for k in range(CC):
                            cpe = E("tensor", lambda e, k=k, hh=hh, bsl=bsl, bkk=bkk, wslb=wslb: e.matmul(
                                pb[bkk][:],
                                wslb[:, k * 512 + hh * 128: k * 512 + hh * 128 + 128],
                                h2sl[bsl][:, k * 512:(k + 1) * 512],
                                start=(k == 0), stop=(k == CC - 1)),
                                wait=(wpe if k == 0 else []),
                                inc=("sPE", 1) if k == CC - 1 else None) or cpe
                        slab[("pe", s)] = cpe
                        ts0 = (tr[0] % 2) * 512
                        tr[0] += 1
                        wst = tvf_free.pop(ts0, [])
                        if which == "q":
                            bias_col, scale, dst = COL_BQ, SC, qD_r
                        elif which == "k":
                            bias_col, scale, dst = COL_BK, 1.0, kD_r
                        else:
                            bias_col, scale, dst = COL_BV, 1.0, None
                        cev = E("scalar", lambda e, bkk=bkk, bias_col=bias_col, hh=hh, scale=scale, ts0=ts0: e.activation(
                            out=tvf[:, ts0:ts0 + 512], in_=pb[bkk][:],
                            func=AF.Identity, scale=scale,
                            bias=colsf[:, bias_col + hh:bias_col + hh + 1]),
                            wait=[("sPE", cpe)] + wst, inc=("sAC", 1))
                        bank_read(("pb", bkk), "sAC", cev)
                        if which in ("q", "k"):
                            qf = "qkA" if ts0 == 0 else "qkB"
                            cst = E("sync", lambda e, dst=dst, hh=hh, s=s, ts0=ts0: e.dma_start(
                                out=dst[:, hh, s * 512:(s + 1) * 512], in_=tvf[:, ts0:ts0 + 512]),
                                wait=[("sAC", cev)], inc=(qf, 16))
                            tvf_free[ts0] = [(qf, cst)]
                        else:
                            vs = (s % 2) * 1024
                            wtr = bank_wait(("ptr", 0)) + [("sAC", cev), ("iniW", 32)]
                            ctr = 0
                            for sub in range(4):
                                ctr = E("tensor", lambda e, sub=sub, ts0=ts0, vs=vs: e.transpose(
                                    ptrA[:, vs + sub * 128: vs + sub * 128 + 128],
                                    tvf[:, ts0 + sub * 128: ts0 + sub * 128 + 128], identb[:]),
                                    wait=(wtr if sub == 0 else []),
                                    inc=("sPE", 1) if sub == 3 else None) or ctr
                            tvf_free[ts0] = [("sPE", ctr)]
                            vs0 = (s % 2) * 2048
                            cve = E("vector", lambda e, vs=vs, vs0=vs0: e.tensor_copy(
                                ptsb[:, vs0:vs0 + 512], ptrA[:, vs:vs + 512]),
                                wait=[("sPE", ctr)] + ptsb_free.pop(vs0, []), inc=("sDV", 1))
                            bank_read(("ptr", 0), "sDV", cve)
                            vf = "vsA" if vs0 == 0 else "vsB"
                            cvs = E("sync", lambda e, hh=hh, s=s, vs0=vs0: e.dma_start(
                                out=vD_r[:, hh, s * 4:(s + 1) * 4, :],
                                in_=ptsb[:, vs0:vs0 + 512].rearrange("p (tb d) -> p tb d", d=128)),
                                wait=[("sDV", cve)], inc=(vf, 16))
                            ptsb_free[vs0] = [(vf, cvs)]
            st["qkv_done"] = [(f, cnt.get(f, 0)) for f in ("qkA", "qkB", "vsA", "vsB")]
            st["a4_free"] = [("sPE", cnt["sPE"])]

        def emit_attn():
            ysb = A1
            qh = [A2[:, 0:4096], A2[:, 4096:8192]]
            kh = [A2[:, 8192:12288], A2[:, 12288:16384]]
            vh = [A2[:, 16384:20480], A2[:, 20480:24576]]
            Pb = [A3, A4[:, 0:16384]]
            yb = [pbv, pbv]
            ptrX = [ptrA, ptrB]
            hload = {}

            def load_head(hh, extra=()):
                bb = hh % 2
                hf = "hlA" if bb == 0 else "hlB"
                w = list(st["qkv_done"]) + list(extra)
                if hh < 2:
                    w += st["a2_free"]
                E("sync", lambda e, hh=hh, bb=bb: e.dma_start(out=qh[bb][:], in_=qD_r[:, hh, :]),
                  wait=w, inc=(hf, 16))
                E("sync", lambda e, hh=hh, bb=bb: e.dma_start(out=kh[bb][:], in_=kD_r[:, hh, :]),
                  inc=(hf, 16))
                hload[hh] = (hf, E("sync", lambda e, hh=hh, bb=bb: e.dma_start(
                    out=vh[bb][:].rearrange("p (tb d) -> p tb d", d=128), in_=vD_r[:, hh, :, :]),
                    inc=(hf, 16)))

            load_head(0)
            load_head(1)
            srot = [0]
            pfree = {0: 0, 1: 0}

            def emit_scores_softmax(hh, g, u):
                bb = hh % 2
                nkb = g + 1
                P = Pb[u]
                ro = u * 32
                co = u * 4
                crs_gp = 0
                czz_u = [cnt["sGP"]]
                for qt in range(QT):
                    cex = 0
                    for kb in range(nkb):
                        wdt = 512 if kb < g else (qt + 1) * 128
                        bkk = srot[0] % 3
                        srot[0] += 1
                        slot = ro + qt * 8 + kb
                        wpe = [hload[hh]] + bank_wait(("pb", bkk))
                        diag = (kb == g)
                        cpe0 = E("tensor", lambda e, qt=qt, kb=kb, wdt=wdt, bkk=bkk, bb=bb, g=g, diag=diag: e.matmul(
                            pb[bkk][:, 0:wdt],
                            qh[bb][:, g * 512 + qt * 128: g * 512 + qt * 128 + 128],
                            kh[bb][:, kb * 512: kb * 512 + wdt],
                            start=True, stop=not diag),
                            wait=wpe, inc=(None if diag else ("sPE", 1)))
                        if diag:
                            cpe = E("tensor", lambda e, qt=qt, bkk=bkk, wdt=wdt: e.matmul(
                                pb[bkk][:, qt * 128:(qt + 1) * 128],
                                identb[:], trib[:],
                                start=False, stop=True),
                                inc=("sPE", 1))
                        else:
                            cpe = cpe0
                        cmx = E("vector", lambda e, bkk=bkk, slot=slot, wdt=wdt: e.reduce_max(
                            nmrow[:, slot:slot + 1], pb[bkk][:, 0:wdt],
                            axis=AX.X, negate=True),
                            wait=[("sPE", cpe)], inc=("sDV", 1))
                        cex = E("scalar", lambda e, bkk=bkk, qt=qt, kb=kb, wdt=wdt, slot=slot, P=P: e.activation(
                            out=P[:, qt * 4096 + kb * 512: qt * 4096 + kb * 512 + wdt],
                            in_=pb[bkk][:, 0:wdt], func=AF.Exp,
                            bias=nmrow[:, slot:slot + 1],
                            accum_out=smrow[:, slot:slot + 1]),
                            wait=[("sDV", cmx), ("sPE", pfree[u])], inc=("sAC", 1))
                        bank_read(("pb", bkk), "sAC", cex)
                    if qt < 3:
                        E("gpsimd", lambda e, qt=qt, g=g, P=P: e.memset(
                            P[:, qt * 4096 + g * 512 + (qt + 1) * 128: qt * 4096 + (g + 1) * 512],
                            0.0), wait=[("sAC", cex), ("sPE", pfree[u])], inc=("sGP", 1))
                    czz_u[0] = cnt["sGP"]
                def nm3(t, w):
                    return t[:, ro:ro + 32].rearrange("p (q k) -> p q k", k=8)[:, :, 0:w]
                ES("vector", lambda e, nkb=nkb, ro=ro, co=co: e.tensor_reduce(
                    nminc[:, co:co + 4], nm3(nmrow, nkb), AX.X, ALU.min),
                    wait=[("sAC", cex)])
                ES("vector", lambda e, nkb=nkb, ro=ro, co=co: e.tensor_tensor(
                    nm3(frow, nkb), nm3(nmrow, nkb), bc0(nminc[:, co:co + 4], nkb),
                    ALU.subtract))
                cfr = ES("scalar", lambda e, nkb=nkb, ro=ro: e.activation(
                    out=nm3(frow, nkb), in_=nm3(frow, nkb), func=AF.Exp, scale=-1.0),
                    wait=[("sDV", cnt["sDV"])])
                ES("vector", lambda e, nkb=nkb, ro=ro: e.tensor_mul(
                    nm3(growc, nkb), nm3(frow, nkb), nm3(smrow, nkb)), wait=[("sAC", cfr)])
                ES("vector", lambda e, nkb=nkb, ro=ro, co=co: e.reduce_sum(
                    scol[:, co:co + 4], nm3(growc, nkb), AX.X))
                ES("vector", lambda e, co=co: e.reciprocal(
                    scol[:, co:co + 4], scol[:, co:co + 4]))
                cgr = ES("vector", lambda e, nkb=nkb, ro=ro, co=co: e.tensor_tensor(
                    nm3(growc, nkb), nm3(frow, nkb), bc0(scol[:, co:co + 4], nkb),
                    ALU.mult))
                crs_gp = cgr
                crs_ac = 0
                for qt in range(QT):
                    for kb in range(nkb):
                        wdt = 512 if kb < g else (qt + 1) * 128
                        if kb % 2 == 0:
                            crs_gp = E("vector", lambda e, qt=qt, kb=kb, wdt=wdt, ro=ro, P=P: e.tensor_scalar(
                                P[:, qt * 4096 + kb * 512: qt * 4096 + kb * 512 + wdt],
                                P[:, qt * 4096 + kb * 512: qt * 4096 + kb * 512 + wdt],
                                growc[:, ro + qt * 8 + kb:ro + qt * 8 + kb + 1], None, ALU.mult),
                                wait=([("sAC", cgr)] if False else []), inc=("sDV", 1))
                        else:
                            crs_ac = E("scalar", lambda e, qt=qt, kb=kb, wdt=wdt, ro=ro, P=P: e.activation(
                                out=P[:, qt * 4096 + kb * 512: qt * 4096 + kb * 512 + wdt],
                                in_=P[:, qt * 4096 + kb * 512: qt * 4096 + kb * 512 + wdt],
                                func=AF.Identity,
                                scale=growc[:, ro + qt * 8 + kb:ro + qt * 8 + kb + 1]),
                                wait=[("sDV", cgr)], inc=("sAC", 1))
                return (crs_gp, crs_ac), czz_u[0]

            def emit_transp_pv(hh, g, u, crs2, czz):
                crs_gp, crs_ac = crs2
                bb = hh % 2
                nkb = g + 1
                P = Pb[u]
                ybk = yb[u]
                cpv = 0
                for kb in range(nkb):
                    pslot = kb % 2
                    ptr = ptrX[pslot]
                    wtr = bank_wait(("ptr", pslot)) + [("sDV", crs_gp), ("sAC", crs_ac), ("sGP", czz)]
                    first = True
                    ctr = 0
                    for qt in range(QT):
                        for sub in range(4):
                            last = (qt == QT - 1 and sub == 3)
                            ctr = E("tensor", lambda e, qt=qt, kb=kb, sub=sub, P=P, ptr=ptr: e.transpose(
                                ptr[:, sub * 512 + qt * 128: sub * 512 + qt * 128 + 128],
                                P[:, qt * 4096 + kb * 512 + sub * 128: qt * 4096 + kb * 512 + sub * 128 + 128],
                                identb[:]),
                                wait=(wtr if first else []),
                                inc=("sPE", 1) if last else None) or ctr
                            first = False
                    ps0 = pslot * 2048
                    pw = ptsb_free.pop(ps0, [])
                    cptA = E("vector", lambda e, ps0=ps0, ptr=ptr: e.tensor_copy(
                        ptsb[:, ps0:ps0 + 1024], ptr[:, 0:1024]),
                        wait=[("sPE", ctr)] + pw, inc=("sDV", 1))
                    cptB = E("scalar", lambda e, ps0=ps0, ptr=ptr: e.activation(
                        out=ptsb[:, ps0 + 1024:ps0 + 2048], in_=ptr[:, 1024:2048], func=AF.Identity),
                        wait=[("sPE", ctr)] + pw, inc=("sAC", 1))
                    bank_read(("ptr", pslot), "sDV", cptA)
                    bank_read(("ptr", pslot), "sAC", cptB)
                    for sub in range(4):
                        wv_ = [("sDV", cptA), ("sAC", cptB)] if sub == 0 else []
                        if kb == 0 and sub == 0:
                            wv_ += bank_wait(("pbv", 0))
                        cpv = E("tensor", lambda e, kb=kb, sub=sub, bb=bb, ps0=ps0, ybk=ybk,
                                lastmm=(kb == nkb - 1 and sub == 3): e.matmul(
                            ybk[:],
                            vh[bb][:, (kb * 4 + sub) * 128:(kb * 4 + sub) * 128 + 128],
                            ptsb[:, ps0 + sub * 512: ps0 + sub * 512 + 512],
                            start=(kb == 0 and sub == 0), stop=lastmm),
                            wait=wv_,
                            inc=("sPE", 1) if sub == 3 else None) or cpv
                    ptsb_free[ps0] = [("sPE", cpv)]
                pfree[u] = cpv
                cye = E("scalar", lambda e, hh=hh, g=g, ybk=ybk: e.activation(
                    out=ysb[:, hh * 4096 + g * 512: hh * 4096 + g * 512 + 512],
                    in_=ybk[:], func=AF.Identity),
                    wait=[("sPE", cpv)], inc=("sAC", 1))
                bank_read(("pbv", 0), "sAC", cye)
                return cpv

            for pair in range(HPC // 2):
                h0, h1 = 2 * pair, 2 * pair + 1
                units = [(h, g) for g in range(NG) for h in (h0, h1)]
                if dbg2 and pair == 0:
                    units = units[:1]
                prev = None
                for uix, (hh, g) in enumerate(units):
                    u = uix % 2
                    crs_gp, czz = emit_scores_softmax(hh, g, u)
                    if prev is not None:
                        emit_transp_pv(*prev)
                    prev = (hh, g, u, crs_gp, czz)
                emit_transp_pv(*prev)
                if dbg2:
                    break
                if 2 * pair + 2 < HPC:
                    snap = cnt["sPE"]
                    load_head(2 * pair + 2, extra=[("sPE", snap)])
                    load_head(2 * pair + 3)
            if dbg2:
                E("sync", lambda e: e.nop(), wait=[("sAC", cnt["sAC"]), ("sDV", cnt["sDV"]),
                                                   ("sPE", cnt["sPE"]), ("sGP", cnt["sGP"])])
                for nm, src in (("dP", A3), ("dnm", nmrow[:, 0:32]), ("dsm", smrow[:, 0:32]),
                                ("dgr", growc[:, 0:32]), ("dsc", scol[:, 0:4]),
                                ("dysb", ysb[:, 0:4096]), ("dpt", ptsb)):
                    E("sync", lambda e, nm=nm, src=src: e.dma_start(out=dbg2[nm], in_=src[:] if hasattr(src, "ap") else src),
                      inc=("dbgf", 16))
            cys = E("sync", lambda e: e.dma_start(
                out=ysrc_r, in_=ysb[:].rearrange("p (h t) -> p h t", h=HPC)),
                wait=[("sAC", cnt["sAC"])], inc=("ysr", 16))
            st["y_cc"] = E("gpsimd", lambda e: e.collective_compute(
                "AllGather", ALU.bypass, RG, ins=[ysrc], outs=[yallD]),
                wait=[("ysr", cys)], inc=("cc2", 1))
            st["ysb_last"] = cys
            st["a2_free"] = [("sPE", cnt["sPE"])]
            st["a4_free"] = [("sPE", cnt["sPE"])]

        def emit_outproj():
            yall = A1
            wslb = A2[:, 0:16384]
            cyl = E("sync", lambda e: e.dma_start(
                out=yall[:].rearrange("p (hc t) -> p hc t", hc=CC),
                in_=yallD_r[:, :, bass.ts(e.partition_id() % 4, 1024)]),
                wait=[("cc2", st["y_cc"]), ("ysr", st["ysb_last"])], inc=("yal", 16))
            rot = [0]
            last_cpe = 0
            for og in range(2):
                if og == 0:
                    w = [("cc2", st["y_cc"])] + st["a2_free"]
                else:
                    w = [("sPE", last_cpe)]
                cwo = E("gpsimd", lambda e, og=og: e.dma_start(
                    out=wslb.rearrange("p (hc c) -> p hc c", c=2048),
                    in_=wo_r[:, og * 8:(og + 1) * 8, :]), wait=w, inc=("wos", 16))
                for ccx in range(CC):
                    for th in range(2):
                        bkk = rot[0] % 4
                        rot[0] += 1
                        wpe = [("wos", cwo), ("yal", cyl)] + bank_wait(("pb", bkk))
                        cpe = 0
                        for k in range(8):
                            cpe = E("tensor", lambda e, k=k, og=og, ccx=ccx, th=th, bkk=bkk: e.matmul(
                                pb[bkk][:],
                                wslb[:, k * 2048 + ccx * 128: k * 2048 + ccx * 128 + 128],
                                yall[:, (og * 8 + k) * 1024 + th * 512: (og * 8 + k) * 1024 + th * 512 + 512],
                                start=(k == 0), stop=(k == 7)),
                                wait=(wpe if k == 0 else []),
                                inc=("sPE", 1) if k == 7 else None) or cpe
                        last_cpe = cpe
                        fs = (rot[0] % 4) * 512
                        wf = tf_free.pop(fs, [])
                        if og == 0:
                            cev = E("scalar", lambda e, bkk=bkk, ccx=ccx, fs=fs: e.activation(
                                out=tf32[:, fs:fs + 512], in_=pb[bkk][:], func=AF.Identity,
                                bias=colsf[:, COL_BO + ccx:COL_BO + ccx + 1]),
                                wait=[("sPE", cpe)] + wf, inc=("sAC", 1))
                        else:
                            cev = E("scalar", lambda e, bkk=bkk, fs=fs: e.activation(
                                out=tf32[:, fs:fs + 512], in_=pb[bkk][:], func=AF.Identity),
                                wait=[("sPE", cpe)] + wf, inc=("sAC", 1))
                        bank_read(("pb", bkk), "sAC", cev)
                        af = "acc%d" % (fs // 512)
                        cdma = E("gpsimd", lambda e, ccx=ccx, th=th, fs=fs: e.dma_start(
                            out=yout_r[:, ccx, th * 512:(th + 1) * 512],
                            in_=tf32[:, fs:fs + 512], accum_op=ALU.add),
                            wait=[("sAC", cev), ("xinit", c_init)], inc=(af, 16))
                        tf_free[fs] = [(af, cdma)]
                        st.setdefault("chunk_done", {}).setdefault(ccx, []).append((af, cdma))
            st["ln_src_wait"] = [("acc%d" % i, cnt.get("acc%d" % i, 0)) for i in range(4)] + \
                [("sPE", cnt["sPE"])]
            st["a2_free"] = [("sPE", cnt["sPE"])]

        xT_r = xT.rearrange("(cc p) t -> p cc t", p=128)

        def mk_chunk_wait():
            cd = st.pop("chunk_done", None)
            pe_snap = st.pop("xb_free_pe", None) or cnt["sPE"]
            if cd is None:
                return None
            return lambda ccx: list(cd[ccx]) + [("sPE", pe_snap)]

        if "ffn1" in phases:
            emit_ln(src_r=xT_r, chunk_wait=lambda ccx: [("sDV", c_cols)])
            emit_ffn()
        if "attn" in phases:
            emit_ln(chunk_wait=mk_chunk_wait())
            emit_qkv()
            emit_attn()
            emit_outproj()
        if "ffn2" in phases:
            emit_ln(chunk_wait=mk_chunk_wait())
            emit_ffn()

        if dbg:
            E("sync", lambda e: e.nop(), wait=[("sPE", cnt["sPE"]),
                                                     ("sDV", cnt["sDV"]), ("sAC", cnt["sAC"])])
            for nm, (dst, src) in dbg.items():
                E("sync", lambda e, dst=dst, src=src: e.dma_start(out=dst, in_=src), inc=("dbgf", 16))
        E("gpsimd", lambda e: e.nop(),
          wait=[(f, v) for f, v in cnt.items() if f not in ("sPE", "sDV", "sAC", "sGP") and v > 0])

        for k in cnt:
            assert maxwait[k] <= cnt[k], (k, maxwait[k], cnt[k])

        @block.tensor
        def _(t):
            for f in sched["tensor"]:
                f(t)

        @block.vector
        def _(v):
            for f in sched["vector"]:
                f(v)

        @block.scalar
        def _(s):
            for f in sched["scalar"]:
                f(s)

        @block.gpsimd
        def _(g):
            for f in sched["gpsimd"]:
                f(g)

        @block.sync
        def _(s):
            for f in sched["sync"]:
                f(s)

    return nc


def make_inputs(L0_attn_x, L0_attn_Wq, L0_attn_bq, L0_attn_Wk, L0_attn_bk,
                L0_attn_Wv, L0_attn_bv, L0_attn_Wo, L0_attn_bo,
                L0_ffn_gate_w, L0_ffn_up_w, L0_ffn_down_w,
                L0_ffn_gate_s, L0_ffn_up_s, L0_ffn_down_s):
    f32 = np.float32
    gwT = np.ascontiguousarray(np.asarray(L0_ffn_gate_w, f32).T)
    uwT = np.ascontiguousarray(np.asarray(L0_ffn_up_w, f32).T)
    dwT = np.ascontiguousarray(np.asarray(L0_ffn_down_w, f32).T)
    woA = np.ascontiguousarray(np.asarray(L0_attn_Wo, f32))
    tri = np.where(np.arange(128)[None, :] <= np.arange(128)[:, None], 0.0, -1e30).astype(f32)
    ident = np.eye(128, dtype=f32)
    Wq = np.asarray(L0_attn_Wq, f32)
    Wk = np.asarray(L0_attn_Wk, f32)
    Wv = np.asarray(L0_attn_Wv, f32)
    bq = np.asarray(L0_attn_bq, f32)
    bk = np.asarray(L0_attn_bk, f32)
    bv = np.asarray(L0_attn_bv, f32)
    x = np.asarray(L0_attn_x, f32)
    gate_s = np.asarray(L0_ffn_gate_s, f32)
    up_s = np.asarray(L0_ffn_up_s, f32)
    down_s = np.asarray(L0_ffn_down_s, f32)
    bo = np.asarray(L0_attn_bo, f32)

    in_maps = []
    for c in range(NCORES):
        b, r = c // 4, c % 4
        hsel = slice(4 * r, 4 * r + 4)
        cols = np.concatenate([
            gate_s.reshape(44, 128), up_s.reshape(44, 128),
            down_s.reshape(16, 128), bo.reshape(16, 128),
            bq[hsel].reshape(4, 128), bk[hsel].reshape(4, 128), bv[hsel].reshape(4, 128),
        ], axis=0)
        in_maps.append(dict(
            xT=np.ascontiguousarray(x[b, r * NT:(r + 1) * NT, :].T),
            gwT=gwT, uwT=uwT, dwT=dwT,
            wq=np.ascontiguousarray(Wq[hsel].transpose(1, 0, 2).reshape(D, 4 * HD)),
            wk=np.ascontiguousarray(Wk[hsel].transpose(1, 0, 2).reshape(D, 4 * HD)),
            wv=np.ascontiguousarray(Wv[hsel].transpose(1, 0, 2).reshape(D, 4 * HD)),
            wo=woA,
            cols=np.ascontiguousarray(cols.T),
            trimask=tri, identf=ident,
        ))
    return in_maps


def kernel(**inputs):
    in_maps = make_inputs(**inputs)
    nc = build_nc()
    res = run_bass_kernel_spmd(nc, in_maps, core_ids=list(range(NCORES)))
    out = np.empty((B, T, D), np.float32)
    for c in range(NCORES):
        b, r = c // 4, c % 4
        out[b, r * NT:(r + 1) * NT, :] = np.asarray(res.results[c]["yout"]).T
    return out


# version 11
# speedup vs baseline: 1.0523x; 1.0523x over previous
import numpy as np
from contextlib import ExitStack
import concourse.bass as bass
import concourse.mybir as mybir
from concourse.bass_utils import run_bass_kernel_spmd

F32 = mybir.dt.float32
BF16 = mybir.dt.bfloat16
AF = mybir.ActivationFunctionType
ALU = mybir.AluOpType
AX = mybir.AxisListType

B, T, D, H, HD, INTER = 2, 4096, 2048, 16, 128, 5632
NCORES, GRP = 8, 4
NT = 1024
CC = D // 128
ICH = INTER // 128
IGRP = 2
ICG = ICH // IGRP
HPC = 4
NG = 8
QT = 4
SC = float(1.0 / np.sqrt(128.0))

RG = [[0, 1, 2, 3], [4, 5, 6, 7]]

COL_GS = 0
COL_US = 44
COL_DS = 88
COL_BO = 104
COL_BQ = 120
COL_BK = 124
COL_BV = 128
NCOLS = 132


def build_nc(phases=("ffn1", "attn", "ffn2")):
    nc = bass.Bass("TRN2", target_bir_lowering=False, num_devices=8)

    xT = nc.dram_tensor("xT", [D, NT], F32, kind="ExternalInput").ap()
    gwT = nc.dram_tensor("gwT", [D, INTER], F32, kind="ExternalInput").ap()
    uwT = nc.dram_tensor("uwT", [D, INTER], F32, kind="ExternalInput").ap()
    dwT = nc.dram_tensor("dwT", [INTER, D], F32, kind="ExternalInput").ap()
    wq = nc.dram_tensor("wq", [D, HPC * HD], F32, kind="ExternalInput").ap()
    wk = nc.dram_tensor("wk", [D, HPC * HD], F32, kind="ExternalInput").ap()
    wv = nc.dram_tensor("wv", [D, HPC * HD], F32, kind="ExternalInput").ap()
    wo = nc.dram_tensor("wo", [D, D], F32, kind="ExternalInput").ap()
    colsD = nc.dram_tensor("cols", [128, NCOLS], F32, kind="ExternalInput").ap()
    triD = nc.dram_tensor("trimask", [128, 128], F32, kind="ExternalInput").ap()
    identD = nc.dram_tensor("identf", [128, 128], F32, kind="ExternalInput").ap()
    yout = nc.dram_tensor("yout", [D, NT], F32, kind="ExternalOutput").ap()

    h2src = nc.dram_tensor("h2src", [D, NT], BF16, kind="Internal").ap()
    h2all = nc.dram_tensor("h2all", [GRP * D, NT], BF16, kind="Internal").ap()
    qD = nc.dram_tensor("qD", [HPC * HD, T], BF16, kind="Internal").ap()
    kD = nc.dram_tensor("kD", [HPC * HD, T], BF16, kind="Internal").ap()
    vD = nc.dram_tensor("vD", [HPC * T, HD], BF16, kind="Internal").ap()
    ysrc = nc.dram_tensor("ysrc", [HPC * HD, T], BF16, kind="Internal").ap()
    yallD = nc.dram_tensor("yallD", [D, T], BF16, kind="Internal").ap()
    dbg = {}
    dbg2 = {}
    if "dbg2" in phases:
        for nm, shp, dt in (("dP", [128, 16384], BF16), ("dnm", [128, 32], F32),
                            ("dsm", [128, 32], F32), ("dgr", [128, 32], F32),
                            ("dsc", [128, 4], F32), ("dysb", [128, 4096], BF16),
                            ("dpt", [128, 4096], BF16)):
            dbg2[nm] = nc.dram_tensor(nm, shp, dt, kind="ExternalOutput").ap()
    if "dbg" in phases:
        for nm, src in (("dq", qD), ("dk", kD), ("dv", vD), ("dy", ysrc), ("dyall", yallD), ("dh2", h2all)):
            dbg[nm] = (nc.dram_tensor(nm, list(src.shape), BF16, kind="ExternalOutput").ap(), src)

    gwT_r = gwT.rearrange("(cc p) i -> p cc i", p=128)
    uwT_r = uwT.rearrange("(cc p) i -> p cc i", p=128)
    dwT_r = dwT.rearrange("(ic p) h -> p ic h", p=128)
    wq_r = wq.rearrange("(cc p) d -> p cc d", p=128)
    wk_r = wk.rearrange("(cc p) d -> p cc d", p=128)
    wv_r = wv.rearrange("(cc p) d -> p cc d", p=128)
    wo_r = wo.rearrange("(hc p) c -> p hc c", p=128)
    yout_r = yout.rearrange("(cc p) t -> p cc t", p=128)
    h2src_r = h2src.rearrange("(cc p) t -> p cc t", p=128)
    h2all_r = h2all.rearrange("(r cc p) t -> p r cc t", p=128, cc=CC)
    qD_r = qD.rearrange("(h p) t -> p h t", p=128)
    kD_r = kD.rearrange("(h p) t -> p h t", p=128)
    vD_r = vD.rearrange("(h tb p) d -> p h tb d", p=128, tb=T // 128)
    ysrc_r = ysrc.rearrange("(h p) t -> p h t", p=128)
    yallD_r = yallD.rearrange("(hc p) t -> p hc t", p=128)

    with ExitStack() as _ctx:
        A1 = _ctx.enter_context(nc.sbuf_tensor("A1", [128, 16384], BF16))
        A2 = _ctx.enter_context(nc.sbuf_tensor("A2", [128, 24576], BF16))
        A3 = _ctx.enter_context(nc.sbuf_tensor("A3", [128, 16384], BF16))
        A4 = _ctx.enter_context(nc.sbuf_tensor("A4", [128, 24576], BF16))
        sqb = _ctx.enter_context(nc.sbuf_tensor("sqb", [128, 2048], BF16))
        tta = _ctx.enter_context(nc.sbuf_tensor("tta", [128, 2048], BF16))
        tvf = _ctx.enter_context(nc.sbuf_tensor("tvf", [128, 1024], BF16))
        ptsb = _ctx.enter_context(nc.sbuf_tensor("ptsb", [128, 4096], BF16))
        tf32 = _ctx.enter_context(nc.sbuf_tensor("tf32", [128, 2048], F32))
        colsf = _ctx.enter_context(nc.sbuf_tensor("colsf", [128, NCOLS], F32))
        trif = _ctx.enter_context(nc.sbuf_tensor("trif", [128, 128], F32))
        identb = _ctx.enter_context(nc.sbuf_tensor("identb", [128, 128], BF16))
        trib = _ctx.enter_context(nc.sbuf_tensor("trib", [128, 128], BF16))
        onesb = _ctx.enter_context(nc.sbuf_tensor("onesb", [128, 1], BF16))
        onesrow = _ctx.enter_context(nc.sbuf_tensor("onesrow", [1, 128], F32))
        epsc = _ctx.enter_context(nc.sbuf_tensor("epsc", [128, 1], F32))
        murow = _ctx.enter_context(nc.sbuf_tensor("murow", [1, 512], F32))
        varrow = _ctx.enter_context(nc.sbuf_tensor("varrow", [1, 512], F32))
        rrow = _ctx.enter_context(nc.sbuf_tensor("rrow", [1, 512], F32))
        murrow = _ctx.enter_context(nc.sbuf_tensor("murrow", [1, 512], F32))
        s1row = _ctx.enter_context(nc.sbuf_tensor("s1row", [1, 512], F32))
        nmrow = _ctx.enter_context(nc.sbuf_tensor("nmrow", [128, 64], F32))
        smrow = _ctx.enter_context(nc.sbuf_tensor("smrow", [128, 64], F32))
        frow = _ctx.enter_context(nc.sbuf_tensor("frow", [128, 64], F32))
        growc = _ctx.enter_context(nc.sbuf_tensor("growc", [128, 64], F32))
        nminc = _ctx.enter_context(nc.sbuf_tensor("nminc", [128, 8], F32))
        scol = _ctx.enter_context(nc.sbuf_tensor("scol", [128, 8], F32))
        pb0 = _ctx.enter_context(nc.psum_tensor("pb0", [128, 512], F32))
        pb1 = _ctx.enter_context(nc.psum_tensor("pb1", [128, 512], F32))
        pb2 = _ctx.enter_context(nc.psum_tensor("pb2", [128, 512], F32))
        pbv = _ctx.enter_context(nc.psum_tensor("pbv", [128, 512], F32))
        ptrB = _ctx.enter_context(nc.psum_tensor("ptrB", [128, 2048], BF16))
        ptrA = _ctx.enter_context(nc.psum_tensor("ptrA", [128, 2048], BF16))
        sPE = _ctx.enter_context(nc.semaphore("sPE"))
        sDV = _ctx.enter_context(nc.semaphore("sDV"))
        sAC = _ctx.enter_context(nc.semaphore("sAC"))
        sGP = _ctx.enter_context(nc.semaphore("sGP"))
        block = _ctx.enter_context(nc.Block())
        pb = [pb0, pb1, pb2, pbv]

        sched = {k: [] for k in ("tensor", "vector", "scalar", "gpsimd", "sync")}
        cnt = {"sPE": 0, "sDV": 0, "sAC": 0, "sGP": 0}
        sem = {"sPE": sPE, "sDV": sDV, "sAC": sAC, "sGP": sGP}
        maxwait = dict(cnt)

        def S(name):
            if name not in sem:
                sem[name] = _ctx.enter_context(nc.semaphore(name))
                cnt[name] = 0
                maxwait[name] = 0
            return sem[name]

        def E(eng, make, wait=(), inc=None):
            wl = []
            for sname, v in wait:
                if v > 0:
                    S(sname)
                    maxwait[sname] = max(maxwait[sname], v)
                    wl.append((sem[sname], v))
            ib = None
            if inc:
                sname, amt = inc
                S(sname)
                cnt[sname] += amt
                ib = (sem[sname], amt)

            def run(e, make=make, wl=wl, ib=ib):
                for s, v in wl:
                    e.wait_ge(s, v)
                ins = make(e)
                if ib is not None:
                    ins.then_inc(*ib)
            sched[eng].append(run)
            return cnt[inc[0]] if inc else None

        pend = {}

        def bank_wait(bid):
            return pend.pop(bid, [])

        def bank_read(bid, sname, val):
            pend.setdefault(bid, []).append((sname, val))

        import dataclasses as _dc

        def bc0(ap, n):
            return _dc.replace(ap, ap=[list(p) for p in ap.ap] + [[0, n]])

        ENGSEM = {"tensor": "sPE", "vector": "sDV", "scalar": "sAC", "gpsimd": "sGP"}

        def ES(eng, make, wait=()):
            sname = ENGSEM[eng]
            v = E(eng, make, wait=wait, inc=(sname, 1))
            maxwait[sname] = max(maxwait[sname], v)

            def w(e, ss=sem[sname], v=v):
                e.wait_ge(ss, v)
            sched[eng].append(w)
            return v

        E("sync", lambda e: e.dma_start(out=colsf[:], in_=colsD), inc=("iniS", 16))
        E("sync", lambda e: e.dma_start(out=trif[:], in_=triD), inc=("iniS", 16))
        c_init = E("sync", lambda e: e.dma_start(out=yout, in_=xT), inc=("xinit", 16))
        E("gpsimd", lambda e: e.dma_start(out=identb[:], in_=identD), inc=("iniW", 16))
        E("gpsimd", lambda e: e.dma_start(out=trib[:], in_=triD), inc=("iniW", 16))
        E("gpsimd", lambda e: e.memset(onesb[:], 1.0), inc=("sGP", 1))
        E("gpsimd", lambda e: e.memset(onesrow[:], 1.0), inc=("sGP", 1))
        c_memset = E("gpsimd", lambda e: e.memset(epsc[:], 1e-5), inc=("sGP", 1))
        E("vector", lambda e: e.tensor_scalar_mul(colsf[:, COL_BQ:COL_BQ + 4],
                                                  colsf[:, COL_BQ:COL_BQ + 4], SC),
          wait=[("iniS", 32)])
        c_cols = E("vector", lambda e: e.tensor_scalar_mul(colsf[:, COL_DS:COL_DS + 16],
                                                           colsf[:, COL_DS:COL_DS + 16], 0.5),
                   inc=("sDV", 1))

        st = {
            "ln_src_wait": [("xinit", c_init), ("sDV", c_cols)],
            "xn_ready": 0,
            "a2_free": [],
            "a4_free": [("iniS", 32), ("iniW", 32)],
            "p_free": 0,
        }
        tvf_free = {}
        ptsb_free = {}
        tf_free = {}

        def emit_ln(src_r=None, chunk_wait=None):
            xb = A1
            src_r_ = yout_r if src_r is None else src_r
            cWs = {}
            stat_pe = {}

            def emit_cast(ccx):
                w = list(st["ln_src_wait"]) if chunk_wait is None else chunk_wait(ccx)
                if ccx >= 4:
                    w = w + [("sPE", stat_pe[ccx - 4])]
                cWs[ccx] = ("lnc%d" % (ccx % 4), E("gpsimd", lambda e, ccx=ccx: e.dma_start(
                    out=xb[:, ccx * 1024:(ccx + 1) * 1024],
                    in_=src_r_[:, ccx, :]), wait=w, inc=("lnc%d" % (ccx % 4), 16)))

            for ccx in range(4):
                emit_cast(ccx)
            sq_done = [0, 0]
            mm_done = [0, 0]
            for ccx in range(CC):
                if ccx >= 1 and ccx + 3 < CC and (ccx + 3) >= 4:
                    emit_cast(ccx + 3)
                par = ccx % 2
                w = [cWs[ccx], ("sGP", c_memset)]
                if ccx >= 2:
                    w.append(("sPE", mm_done[par]))
                sq_done[par] = E("scalar", lambda e, ccx=ccx, par=par: e.activation(
                    out=sqb[:, par * 1024:(par + 1) * 1024],
                    in_=xb[:, ccx * 1024:(ccx + 1) * 1024], func=AF.Square),
                    wait=w, inc=("sAC", 1))
                for th in range(2):
                    w = [cWs[ccx]]
                    if ccx == 0:
                        w += [("sGP", c_memset)] + bank_wait(("pb", th))
                    E("tensor", lambda e, ccx=ccx, th=th: e.matmul(
                        pb[th][0:1, :], onesb[:],
                        xb[:, ccx * 1024 + th * 512: ccx * 1024 + th * 512 + 512],
                        start=(ccx == 0), stop=(ccx == CC - 1)), wait=w)
                for th in range(2):
                    w = [("sAC", sq_done[par])]
                    if ccx == 0:
                        w += bank_wait(("pb", 2 + th))
                    mm_done[par] = E("tensor", lambda e, ccx=ccx, th=th, par=par: e.matmul(
                        pb[2 + th][0:1, :], onesb[:],
                        sqb[:, par * 1024 + th * 512: par * 1024 + th * 512 + 512],
                        start=(ccx == 0), stop=(ccx == CC - 1)),
                        wait=w, inc=("sPE", 1))
                stat_pe[ccx] = mm_done[par]
            st["ln_stat_pe"] = stat_pe
            c_stats = mm_done[(CC - 1) % 2]
            cn = 0
            for th in range(2):
                ES("vector", lambda e, th=th: e.tensor_scalar_mul(
                    murow[0:1, :], pb[th][0:1, :], 1.0 / D),
                    wait=[("sPE", c_stats)] + ([("sPE", cnt["sPE"])] if th else []))
                ES("vector", lambda e: e.tensor_mul(
                    s1row[0:1, :], murow[0:1, :], murow[0:1, :]))
                cv = ES("vector", lambda e, th=th: e.scalar_tensor_tensor(
                    varrow[0:1, :], pb[2 + th][0:1, :], 1.0 / D, s1row[0:1, :],
                    ALU.mult, ALU.subtract))
                ES("scalar", lambda e: e.activation(
                    out=rrow[0:1, :], in_=varrow[0:1, :], func=AF.Ln, bias=epsc[0:1, :]),
                    wait=[("sDV", cv)])
                ca = ES("scalar", lambda e: e.activation(
                    out=rrow[0:1, :], in_=rrow[0:1, :], func=AF.Exp, scale=-0.5))
                cmu = ES("vector", lambda e: e.tensor_mul(
                    murrow[0:1, :], murow[0:1, :], rrow[0:1, :]),
                    wait=[("sAC", ca)])
                E("tensor", lambda e, th=th: e.matmul(
                    pb[th][:], onesrow[:], rrow[0:1, :], start=True, stop=True),
                    wait=[("sDV", cmu)])
                cbc = E("tensor", lambda e, th=th: e.matmul(
                    pb[2 + th][:], onesrow[:], murrow[0:1, :], start=True, stop=True),
                    inc=("sPE", 1))
                for ccx in range(CC):
                    s0 = ccx * 1024 + th * 512
                    ES("vector", lambda e, s0=s0, th=th: e.tensor_tensor(
                        xb[:, s0:s0 + 512], xb[:, s0:s0 + 512], pb[th][:], ALU.mult),
                        wait=[("sPE", cbc)] if ccx == 0 else [])
                    cn = E("vector", lambda e, s0=s0, th=th: e.tensor_tensor(
                        xb[:, s0:s0 + 512], xb[:, s0:s0 + 512], pb[2 + th][:], ALU.subtract),
                        inc=("sDV", 1))
                bank_read(("pb", th), "sDV", cn)
                bank_read(("pb", 2 + th), "sDV", cn)
            st["xn_ready"] = cn

        def emit_ffn():
            xb = A1
            hdn = A2
            gsl = [A4[:, 0:4096], A4[:, 4096:8192]]
            usl = [A4[:, 8192:12288], A4[:, 12288:16384]]
            dsl = [A4[:, 16384:16384 + 2816], A4[:, 16384 + 2816:16384 + 5632]]
            rot = [0]
            ttx = [0]
            last_hdn = 0
            for grp in range(IGRP):
                slab = {}
                for j in range(11):
                    if j == 0:
                        c0 = grp * (ICG * 128)
                        w = []
                        if grp == 0:
                            w += st["a4_free"]
                        else:
                            w.append(("sPE", cnt["sPE"]))
                        slab[(0, "g")] = ("gsA", E("gpsimd", lambda e, c0=c0: e.dma_start(
                            out=gsl[0].rearrange("p (cc i) -> p cc i", i=256),
                            in_=gwT_r[:, :, c0:c0 + 256]), wait=w, inc=("gsA", 16)))
                        slab[(0, "u")] = ("usA", E("gpsimd", lambda e, c0=c0: e.dma_start(
                            out=usl[0].rearrange("p (cc i) -> p cc i", i=256),
                            in_=uwT_r[:, :, c0:c0 + 256]), inc=("usA", 16)))
                    if j + 1 < 11:
                        c1 = grp * (ICG * 128) + (j + 1) * 256
                        b1 = (j + 1) % 2
                        w = [("sPE", slab.get((j - 1, "pe"), 0))]
                        gf = "gsA" if b1 == 0 else "gsB"
                        uf = "usA" if b1 == 0 else "usB"
                        slab[(j + 1, "g")] = (gf, E("gpsimd", lambda e, c1=c1, b1=b1: e.dma_start(
                            out=gsl[b1].rearrange("p (cc i) -> p cc i", i=256),
                            in_=gwT_r[:, :, c1:c1 + 256]), wait=w, inc=(gf, 16)))
                        slab[(j + 1, "u")] = (uf, E("gpsimd", lambda e, c1=c1, b1=b1: e.dma_start(
                            out=usl[b1].rearrange("p (cc i) -> p cc i", i=256),
                            in_=uwT_r[:, :, c1:c1 + 256]), inc=(uf, 16)))
                    bsl = j % 2
                    for icl in range(2):
                        icg = j * 2 + icl
                        ica = grp * ICG + icg
                        for th in range(2):
                            bg = rot[0] % 4
                            bu = (rot[0] + 1) % 4
                            rot[0] += 2
                            wpe = [slab[(j, "g")], slab[(j, "u")], ("sDV", st["xn_ready"])]
                            wpe += bank_wait(("pb", bg)) + bank_wait(("pb", bu))
                            for k in range(CC):
                                E("tensor", lambda e, k=k, icl=icl, th=th, bsl=bsl, bg=bg: e.matmul(
                                    pb[bg][:],
                                    gsl[bsl][:, k * 256 + icl * 128: k * 256 + icl * 128 + 128],
                                    xb[:, k * 1024 + th * 512: k * 1024 + th * 512 + 512],
                                    start=(k == 0), stop=(k == CC - 1)),
                                    wait=(wpe if k == 0 else []))
                            cpe = 0
                            for k in range(CC):
                                cpe = E("tensor", lambda e, k=k, icl=icl, th=th, bsl=bsl, bu=bu: e.matmul(
                                    pb[bu][:],
                                    usl[bsl][:, k * 256 + icl * 128: k * 256 + icl * 128 + 128],
                                    xb[:, k * 1024 + th * 512: k * 1024 + th * 512 + 512],
                                    start=(k == 0), stop=(k == CC - 1)),
                                    inc=("sPE", 1) if k == CC - 1 else None) or cpe
                            slab[(j, "pe")] = cpe
                            st["xb_free_pe"] = cpe
                            st["gu_done_pe"] = cpe
                            t0 = (ttx[0] % 2) * 1024
                            ttx[0] += 1
                            wtt = tvf_free.pop(("tta", t0), [])
                            ca1 = E("scalar", lambda e, bg=bg, ica=ica, t0=t0: e.activation(
                                out=tta[:, t0:t0 + 512], in_=pb[bg][:], func=AF.Silu,
                                scale=colsf[:, COL_GS + ica:COL_GS + ica + 1]),
                                wait=[("sPE", cpe)] + wtt, inc=("sAC", 1))
                            ca2 = E("scalar", lambda e, bu=bu, ica=ica, t0=t0: e.activation(
                                out=tta[:, t0 + 512:t0 + 1024], in_=pb[bu][:], func=AF.Identity,
                                scale=colsf[:, COL_US + ica:COL_US + ica + 1]),
                                inc=("sAC", 1))
                            bank_read(("pb", bg), "sAC", ca1)
                            bank_read(("pb", bu), "sAC", ca2)
                            hs = icg * 1024 + th * 512
                            last_hdn = E("vector", lambda e, hs=hs, t0=t0: e.tensor_mul(
                                hdn[:, hs:hs + 512], tta[:, t0:t0 + 512], tta[:, t0 + 512:t0 + 1024]),
                                wait=[("sAC", ca2)], inc=("sDV", 1))
                            tvf_free[("tta", t0)] = [("sDV", last_hdn)]
                dd = {}
                for hc in range(16):
                    if hc == 0:
                        w = [("sPE", cnt["sPE"])] if grp > 0 else []
                        dd[0] = ("dsA", E("gpsimd", lambda e, grp=grp: e.dma_start(
                            out=dsl[0].rearrange("p (ic h) -> p ic h", h=128),
                            in_=dwT_r[:, grp * ICG:(grp + 1) * ICG, 0:128]),
                            wait=w, inc=("dsA", 16)))
                    if hc + 1 < 16:
                        b1 = (hc + 1) % 2
                        df = "dsA" if b1 == 0 else "dsB"
                        dd[hc + 1] = (df, E("gpsimd", lambda e, grp=grp, hc=hc, b1=b1: e.dma_start(
                            out=dsl[b1].rearrange("p (ic h) -> p ic h", h=128),
                            in_=dwT_r[:, grp * ICG:(grp + 1) * ICG, (hc + 1) * 128:(hc + 2) * 128]),
                            wait=[("sPE", dd.get(("pe", hc - 1), 0))], inc=(df, 16)))
                    bsl = hc % 2
                    for th in range(2):
                        bk = rot[0] % 4
                        rot[0] += 1
                        wpe = [dd[hc], ("sDV", last_hdn)] + bank_wait(("pb", bk))
                        cpe = 0
                        for k in range(ICG):
                            cpe = E("tensor", lambda e, k=k, th=th, bsl=bsl, bk=bk: e.matmul(
                                pb[bk][:],
                                dsl[bsl][:, k * 128:k * 128 + 128],
                                hdn[:, k * 1024 + th * 512: k * 1024 + th * 512 + 512],
                                start=(k == 0), stop=(k == ICG - 1)),
                                wait=(wpe if k == 0 else []),
                                inc=("sPE", 1) if k == ICG - 1 else None) or cpe
                        dd[("pe", hc)] = cpe
                        fs = (rot[0] % 4) * 512
                        wf = tf_free.pop(fs, [])
                        cev = E("scalar", lambda e, bk=bk, hc=hc, fs=fs: e.activation(
                            out=tf32[:, fs:fs + 512], in_=pb[bk][:], func=AF.Identity,
                            scale=colsf[:, COL_DS + hc:COL_DS + hc + 1]),
                            wait=[("sPE", cpe)] + wf, inc=("sAC", 1))
                        bank_read(("pb", bk), "sAC", cev)
                        af = "acc%d" % (fs // 512)
                        cdma = E("gpsimd", lambda e, hc=hc, th=th, fs=fs: e.dma_start(
                            out=yout_r[:, hc, th * 512:(th + 1) * 512],
                            in_=tf32[:, fs:fs + 512], accum_op=ALU.add),
                            wait=[("sAC", cev), ("xinit", c_init)], inc=(af, 16))
                        tf_free[fs] = [(af, cdma)]
                        st.setdefault("chunk_done", {}).setdefault(hc, []).append((af, cdma))
            st["ln_src_wait"] = [("acc%d" % i, cnt.get("acc%d" % i, 0)) for i in range(4)] + \
                [("sPE", cnt["sPE"])]
            st["a2_free"] = [("sPE", cnt["sPE"])]
            st["a4_free"] = [("sPE", cnt["sPE"])]

        def emit_qkv():
            xb = A1
            cS = E("sync", lambda e: e.dma_start(
                out=h2src_r, in_=xb[:].rearrange("p (cc t) -> p cc t", cc=CC)),
                wait=[("sDV", st["xn_ready"])], inc=("h2s", 16))
            ccc = E("gpsimd", lambda e: e.collective_compute(
                "AllGather", ALU.bypass, RG, ins=[h2src], outs=[h2all]),
                wait=[("h2s", cS)], inc=("cc1", 1))
            wqb = A4[:, 0:8192]
            wkb = A4[:, 8192:16384]
            wvb = A4[:, 16384:24576]
            E("gpsimd", lambda e: e.dma_start(
                out=wqb.rearrange("p (cc d) -> p cc d", d=512), in_=wq_r),
                wait=[("sPE", st.get("gu_done_pe", 0))], inc=("wqv", 16))
            E("gpsimd", lambda e: e.dma_start(
                out=wkb.rearrange("p (cc d) -> p cc d", d=512), in_=wk_r), inc=("wqv", 16))
            cw = E("gpsimd", lambda e: e.dma_start(
                out=wvb.rearrange("p (cc d) -> p cc d", d=512), in_=wv_r),
                wait=st["a4_free"], inc=("wqv", 16))
            h2sl = [A3[:, 0:8192], A3[:, 8192:16384]]
            slab = {}
            rot = [0]
            tr = [0]
            for s in range(8):
                bsl = s % 2
                wsl = [("cc1", ccc)] if s < 2 else [("sPE", slab[("pe", s - 2)])]
                hf = "h2A" if bsl == 0 else "h2B"
                slab[s] = (hf, E("sync", lambda e, s=s, bsl=bsl: e.dma_start(
                    out=h2sl[bsl].rearrange("p (cc t) -> p cc t", t=512),
                    in_=h2all_r[:, s // 2, :, (s % 2) * 512:(s % 2) * 512 + 512]),
                    wait=wsl, inc=(hf, 16)))
                for hh in range(HPC):
                    for which in ("q", "k", "v"):
                        wslb = {"q": wqb, "k": wkb, "v": wvb}[which]
                        bkk = rot[0] % 4
                        rot[0] += 1
                        wpe = [slab[s], ("wqv", cw)] + bank_wait(("pb", bkk))
                        cpe = 0
                        for k in range(CC):
                            cpe = E("tensor", lambda e, k=k, hh=hh, bsl=bsl, bkk=bkk, wslb=wslb: e.matmul(
                                pb[bkk][:],
                                wslb[:, k * 512 + hh * 128: k * 512 + hh * 128 + 128],
                                h2sl[bsl][:, k * 512:(k + 1) * 512],
                                start=(k == 0), stop=(k == CC - 1)),
                                wait=(wpe if k == 0 else []),
                                inc=("sPE", 1) if k == CC - 1 else None) or cpe
                        slab[("pe", s)] = cpe
                        ts0 = (tr[0] % 2) * 512
                        tr[0] += 1
                        wst = tvf_free.pop(ts0, [])
                        if which == "q":
                            bias_col, scale, dst = COL_BQ, SC, qD_r
                        elif which == "k":
                            bias_col, scale, dst = COL_BK, 1.0, kD_r
                        else:
                            bias_col, scale, dst = COL_BV, 1.0, None
                        cev = E("scalar", lambda e, bkk=bkk, bias_col=bias_col, hh=hh, scale=scale, ts0=ts0: e.activation(
                            out=tvf[:, ts0:ts0 + 512], in_=pb[bkk][:],
                            func=AF.Identity, scale=scale,
                            bias=colsf[:, bias_col + hh:bias_col + hh + 1]),
                            wait=[("sPE", cpe)] + wst, inc=("sAC", 1))
                        bank_read(("pb", bkk), "sAC", cev)
                        if which in ("q", "k"):
                            qf = "qkA" if ts0 == 0 else "qkB"
                            cst = E("sync", lambda e, dst=dst, hh=hh, s=s, ts0=ts0: e.dma_start(
                                out=dst[:, hh, s * 512:(s + 1) * 512], in_=tvf[:, ts0:ts0 + 512]),
                                wait=[("sAC", cev)], inc=(qf, 16))
                            tvf_free[ts0] = [(qf, cst)]
                        else:
                            vs = (s % 2) * 1024
                            wtr = bank_wait(("ptr", 0)) + [("sAC", cev), ("iniW", 32)]
                            ctr = 0
                            for sub in range(4):
                                ctr = E("tensor", lambda e, sub=sub, ts0=ts0, vs=vs: e.transpose(
                                    ptrA[:, vs + sub * 128: vs + sub * 128 + 128],
                                    tvf[:, ts0 + sub * 128: ts0 + sub * 128 + 128], identb[:]),
                                    wait=(wtr if sub == 0 else []),
                                    inc=("sPE", 1) if sub == 3 else None) or ctr
                            tvf_free[ts0] = [("sPE", ctr)]
                            vs0 = (s % 2) * 2048
                            cve = E("vector", lambda e, vs=vs, vs0=vs0: e.tensor_copy(
                                ptsb[:, vs0:vs0 + 512], ptrA[:, vs:vs + 512]),
                                wait=[("sPE", ctr)] + ptsb_free.pop(vs0, []), inc=("sDV", 1))
                            bank_read(("ptr", 0), "sDV", cve)
                            vf = "vsA" if vs0 == 0 else "vsB"
                            cvs = E("sync", lambda e, hh=hh, s=s, vs0=vs0: e.dma_start(
                                out=vD_r[:, hh, s * 4:(s + 1) * 4, :],
                                in_=ptsb[:, vs0:vs0 + 512].rearrange("p (tb d) -> p tb d", d=128)),
                                wait=[("sDV", cve)], inc=(vf, 16))
                            ptsb_free[vs0] = [(vf, cvs)]
            st["qkv_done"] = [(f, cnt.get(f, 0)) for f in ("qkA", "qkB", "vsA", "vsB")]
            st["a4_free"] = [("sPE", cnt["sPE"])]

        def emit_attn():
            ysb = A1
            qh = [A2[:, 0:4096], A2[:, 4096:8192]]
            kh = [A2[:, 8192:12288], A2[:, 12288:16384]]
            vh = [A2[:, 16384:20480], A2[:, 20480:24576]]
            Pb = [A3, A4[:, 0:16384]]
            yb = [pbv, pbv]
            ptrX = [ptrA, ptrB]
            hload = {}

            def load_head(hh, extra=()):
                bb = hh % 2
                hf = "hlA" if bb == 0 else "hlB"
                w = list(st["qkv_done"]) + list(extra)
                if hh < 2:
                    w += st["a2_free"]
                E("sync", lambda e, hh=hh, bb=bb: e.dma_start(out=qh[bb][:], in_=qD_r[:, hh, :]),
                  wait=w, inc=(hf, 16))
                E("sync", lambda e, hh=hh, bb=bb: e.dma_start(out=kh[bb][:], in_=kD_r[:, hh, :]),
                  inc=(hf, 16))
                hload[hh] = (hf, E("sync", lambda e, hh=hh, bb=bb: e.dma_start(
                    out=vh[bb][:].rearrange("p (tb d) -> p tb d", d=128), in_=vD_r[:, hh, :, :]),
                    inc=(hf, 16)))

            load_head(0)
            load_head(1)
            srot = [0]
            pfree = {0: 0, 1: 0}

            def gen_scores_softmax(hh, g, u, out):
                bb = hh % 2
                nkb = g + 1
                P = Pb[u]
                ro = u * 32
                co = u * 4
                crs_gp = 0
                czz_u = [cnt["sGP"]]
                for qt in range(QT):
                    cex = 0
                    for kb in range(nkb):
                        wdt = 512 if kb < g else (qt + 1) * 128
                        bkk = srot[0] % 3
                        srot[0] += 1
                        slot = ro + qt * 8 + kb
                        wpe = [hload[hh]] + bank_wait(("pb", bkk))
                        diag = (kb == g)
                        cpe0 = E("tensor", lambda e, qt=qt, kb=kb, wdt=wdt, bkk=bkk, bb=bb, g=g, diag=diag: e.matmul(
                            pb[bkk][:, 0:wdt],
                            qh[bb][:, g * 512 + qt * 128: g * 512 + qt * 128 + 128],
                            kh[bb][:, kb * 512: kb * 512 + wdt],
                            start=True, stop=not diag),
                            wait=wpe, inc=(None if diag else ("sPE", 1)))
                        if diag:
                            cpe = E("tensor", lambda e, qt=qt, bkk=bkk, wdt=wdt: e.matmul(
                                pb[bkk][:, qt * 128:(qt + 1) * 128],
                                identb[:], trib[:],
                                start=False, stop=True),
                                inc=("sPE", 1))
                        else:
                            cpe = cpe0
                        cmx = E("vector", lambda e, bkk=bkk, slot=slot, wdt=wdt: e.reduce_max(
                            nmrow[:, slot:slot + 1], pb[bkk][:, 0:wdt],
                            axis=AX.X, negate=True),
                            wait=[("sPE", cpe)], inc=("sDV", 1))
                        cex = E("scalar", lambda e, bkk=bkk, qt=qt, kb=kb, wdt=wdt, slot=slot, P=P: e.activation(
                            out=P[:, qt * 4096 + kb * 512: qt * 4096 + kb * 512 + wdt],
                            in_=pb[bkk][:, 0:wdt], func=AF.Exp,
                            bias=nmrow[:, slot:slot + 1],
                            accum_out=smrow[:, slot:slot + 1]),
                            wait=[("sDV", cmx), ("sPE", pfree[u])], inc=("sAC", 1))
                        bank_read(("pb", bkk), "sAC", cex)
                        yield
                    if qt < 3:
                        E("gpsimd", lambda e, qt=qt, g=g, P=P: e.memset(
                            P[:, qt * 4096 + g * 512 + (qt + 1) * 128: qt * 4096 + (g + 1) * 512],
                            0.0), wait=[("sAC", cex), ("sPE", pfree[u])], inc=("sGP", 1))
                    czz_u[0] = cnt["sGP"]
                def nm3(t, w):
                    return t[:, ro:ro + 32].rearrange("p (q k) -> p q k", k=8)[:, :, 0:w]
                ES("vector", lambda e, nkb=nkb, ro=ro, co=co: e.tensor_reduce(
                    nminc[:, co:co + 4], nm3(nmrow, nkb), AX.X, ALU.min),
                    wait=[("sAC", cex)])
                ES("vector", lambda e, nkb=nkb, ro=ro, co=co: e.tensor_tensor(
                    nm3(frow, nkb), nm3(nmrow, nkb), bc0(nminc[:, co:co + 4], nkb),
                    ALU.subtract))
                cfr = ES("scalar", lambda e, nkb=nkb, ro=ro: e.activation(
                    out=nm3(frow, nkb), in_=nm3(frow, nkb), func=AF.Exp, scale=-1.0),
                    wait=[("sDV", cnt["sDV"])])
                ES("vector", lambda e, nkb=nkb, ro=ro: e.tensor_mul(
                    nm3(growc, nkb), nm3(frow, nkb), nm3(smrow, nkb)), wait=[("sAC", cfr)])
                ES("vector", lambda e, nkb=nkb, ro=ro, co=co: e.reduce_sum(
                    scol[:, co:co + 4], nm3(growc, nkb), AX.X))
                ES("vector", lambda e, co=co: e.reciprocal(
                    scol[:, co:co + 4], scol[:, co:co + 4]))
                cgr = ES("vector", lambda e, nkb=nkb, ro=ro, co=co: e.tensor_tensor(
                    nm3(growc, nkb), nm3(frow, nkb), bc0(scol[:, co:co + 4], nkb),
                    ALU.mult))
                crs_gp = cgr
                crs_ac = 0
                for qt in range(QT):
                    for kb in range(nkb):
                        wdt = 512 if kb < g else (qt + 1) * 128
                        if kb % 2 == 0:
                            crs_gp = E("vector", lambda e, qt=qt, kb=kb, wdt=wdt, ro=ro, P=P: e.tensor_scalar(
                                P[:, qt * 4096 + kb * 512: qt * 4096 + kb * 512 + wdt],
                                P[:, qt * 4096 + kb * 512: qt * 4096 + kb * 512 + wdt],
                                growc[:, ro + qt * 8 + kb:ro + qt * 8 + kb + 1], None, ALU.mult),
                                wait=([("sAC", cgr)] if False else []), inc=("sDV", 1))
                        else:
                            crs_ac = E("scalar", lambda e, qt=qt, kb=kb, wdt=wdt, ro=ro, P=P: e.activation(
                                out=P[:, qt * 4096 + kb * 512: qt * 4096 + kb * 512 + wdt],
                                in_=P[:, qt * 4096 + kb * 512: qt * 4096 + kb * 512 + wdt],
                                func=AF.Identity,
                                scale=growc[:, ro + qt * 8 + kb:ro + qt * 8 + kb + 1]),
                                wait=[("sDV", cgr)], inc=("sAC", 1))
                out[0] = ((crs_gp, crs_ac), czz_u[0])
                yield

            def gen_transp_pv(hh, g, u, crs2, czz):
                crs_gp, crs_ac = crs2
                bb = hh % 2
                nkb = g + 1
                P = Pb[u]
                ybk = yb[u]
                cpv = 0
                for kb in range(nkb):
                    pslot = kb % 2
                    ptr = ptrX[pslot]
                    wtr = bank_wait(("ptr", pslot)) + [("sDV", crs_gp), ("sAC", crs_ac), ("sGP", czz)]
                    first = True
                    ctr = 0
                    for qt in range(QT):
                        for sub in range(4):
                            last = (qt == QT - 1 and sub == 3)
                            ctr = E("tensor", lambda e, qt=qt, kb=kb, sub=sub, P=P, ptr=ptr: e.transpose(
                                ptr[:, sub * 512 + qt * 128: sub * 512 + qt * 128 + 128],
                                P[:, qt * 4096 + kb * 512 + sub * 128: qt * 4096 + kb * 512 + sub * 128 + 128],
                                identb[:]),
                                wait=(wtr if first else []),
                                inc=("sPE", 1) if last else None) or ctr
                            first = False
                    ps0 = pslot * 2048
                    pw = ptsb_free.pop(ps0, [])
                    cptA = E("vector", lambda e, ps0=ps0, ptr=ptr: e.tensor_copy(
                        ptsb[:, ps0:ps0 + 1024], ptr[:, 0:1024]),
                        wait=[("sPE", ctr)] + pw, inc=("sDV", 1))
                    cptB = E("scalar", lambda e, ps0=ps0, ptr=ptr: e.activation(
                        out=ptsb[:, ps0 + 1024:ps0 + 2048], in_=ptr[:, 1024:2048], func=AF.Identity),
                        wait=[("sPE", ctr)] + pw, inc=("sAC", 1))
                    bank_read(("ptr", pslot), "sDV", cptA)
                    bank_read(("ptr", pslot), "sAC", cptB)
                    for sub in range(4):
                        wv_ = [("sDV", cptA), ("sAC", cptB)] if sub == 0 else []
                        if kb == 0 and sub == 0:
                            wv_ += bank_wait(("pbv", 0))
                        cpv = E("tensor", lambda e, kb=kb, sub=sub, bb=bb, ps0=ps0, ybk=ybk,
                                lastmm=(kb == nkb - 1 and sub == 3): e.matmul(
                            ybk[:],
                            vh[bb][:, (kb * 4 + sub) * 128:(kb * 4 + sub) * 128 + 128],
                            ptsb[:, ps0 + sub * 512: ps0 + sub * 512 + 512],
                            start=(kb == 0 and sub == 0), stop=lastmm),
                            wait=wv_,
                            inc=("sPE", 1) if sub == 3 else None) or cpv
                    ptsb_free[ps0] = [("sPE", cpv)]
                    yield
                pfree[u] = cpv
                cye = E("scalar", lambda e, hh=hh, g=g, ybk=ybk: e.activation(
                    out=ysb[:, hh * 4096 + g * 512: hh * 4096 + g * 512 + 512],
                    in_=ybk[:], func=AF.Identity),
                    wait=[("sPE", cpv)], inc=("sAC", 1))
                bank_read(("pbv", 0), "sAC", cye)
                return cpv

            for pair in range(HPC // 2):
                h0, h1 = 2 * pair, 2 * pair + 1
                units = [(h, g) for g in range(NG) for h in (h0, h1)]
                if dbg2 and pair == 0:
                    units = units[:1]
                prev = None
                for uix, (hh, g) in enumerate(units):
                    u = uix % 2
                    box = [None]
                    gs = gen_scores_softmax(hh, g, u, box)
                    gt = list() if prev is None else [gen_transp_pv(*prev)]
                    ns = 4 * (g + 1) + 1
                    nt = (prev[1] + 1) if prev else 0
                    ti = 0
                    for si in range(ns):
                        try:
                            next(gs)
                        except StopIteration:
                            pass
                        want_t = ((si + 1) * nt) // ns if nt else 0
                        while gt and ti < want_t:
                            try:
                                next(gt[0])
                                ti += 1
                            except StopIteration:
                                gt = []
                    for _ in gs:
                        pass
                    if gt:
                        for _ in gt[0]:
                            pass
                    crs_gp, czz = box[0]
                    prev = (hh, g, u, crs_gp, czz)
                gtl = gen_transp_pv(*prev)
                for _ in gtl:
                    pass
                if dbg2:
                    break
                if 2 * pair + 2 < HPC:
                    snap = cnt["sPE"]
                    load_head(2 * pair + 2, extra=[("sPE", snap)])
                    load_head(2 * pair + 3)
            if dbg2:
                E("sync", lambda e: e.nop(), wait=[("sAC", cnt["sAC"]), ("sDV", cnt["sDV"]),
                                                   ("sPE", cnt["sPE"]), ("sGP", cnt["sGP"])])
                for nm, src in (("dP", A3), ("dnm", nmrow[:, 0:32]), ("dsm", smrow[:, 0:32]),
                                ("dgr", growc[:, 0:32]), ("dsc", scol[:, 0:4]),
                                ("dysb", ysb[:, 0:4096]), ("dpt", ptsb)):
                    E("sync", lambda e, nm=nm, src=src: e.dma_start(out=dbg2[nm], in_=src[:] if hasattr(src, "ap") else src),
                      inc=("dbgf", 16))
            cys = E("sync", lambda e: e.dma_start(
                out=ysrc_r, in_=ysb[:].rearrange("p (h t) -> p h t", h=HPC)),
                wait=[("sAC", cnt["sAC"])], inc=("ysr", 16))
            st["y_cc"] = E("gpsimd", lambda e: e.collective_compute(
                "AllGather", ALU.bypass, RG, ins=[ysrc], outs=[yallD]),
                wait=[("ysr", cys)], inc=("cc2", 1))
            st["ysb_last"] = cys
            st["a2_free"] = [("sPE", cnt["sPE"])]
            st["a4_free"] = [("sPE", cnt["sPE"])]

        def emit_outproj():
            yall = A1
            wslb = A2[:, 0:16384]
            cyl = E("sync", lambda e: e.dma_start(
                out=yall[:].rearrange("p (hc t) -> p hc t", hc=CC),
                in_=yallD_r[:, :, bass.ts(e.partition_id() % 4, 1024)]),
                wait=[("cc2", st["y_cc"]), ("ysr", st["ysb_last"])], inc=("yal", 16))
            rot = [0]
            last_cpe = 0
            for og in range(2):
                if og == 0:
                    w = [("cc2", st["y_cc"])] + st["a2_free"]
                else:
                    w = [("sPE", last_cpe)]
                cwo = E("gpsimd", lambda e, og=og: e.dma_start(
                    out=wslb.rearrange("p (hc c) -> p hc c", c=2048),
                    in_=wo_r[:, og * 8:(og + 1) * 8, :]), wait=w, inc=("wos", 16))
                for ccx in range(CC):
                    for th in range(2):
                        bkk = rot[0] % 4
                        rot[0] += 1
                        wpe = [("wos", cwo), ("yal", cyl)] + bank_wait(("pb", bkk))
                        cpe = 0
                        for k in range(8):
                            cpe = E("tensor", lambda e, k=k, og=og, ccx=ccx, th=th, bkk=bkk: e.matmul(
                                pb[bkk][:],
                                wslb[:, k * 2048 + ccx * 128: k * 2048 + ccx * 128 + 128],
                                yall[:, (og * 8 + k) * 1024 + th * 512: (og * 8 + k) * 1024 + th * 512 + 512],
                                start=(k == 0), stop=(k == 7)),
                                wait=(wpe if k == 0 else []),
                                inc=("sPE", 1) if k == 7 else None) or cpe
                        last_cpe = cpe
                        fs = (rot[0] % 4) * 512
                        wf = tf_free.pop(fs, [])
                        if og == 0:
                            cev = E("scalar", lambda e, bkk=bkk, ccx=ccx, fs=fs: e.activation(
                                out=tf32[:, fs:fs + 512], in_=pb[bkk][:], func=AF.Identity,
                                bias=colsf[:, COL_BO + ccx:COL_BO + ccx + 1]),
                                wait=[("sPE", cpe)] + wf, inc=("sAC", 1))
                        else:
                            cev = E("scalar", lambda e, bkk=bkk, fs=fs: e.activation(
                                out=tf32[:, fs:fs + 512], in_=pb[bkk][:], func=AF.Identity),
                                wait=[("sPE", cpe)] + wf, inc=("sAC", 1))
                        bank_read(("pb", bkk), "sAC", cev)
                        af = "acc%d" % (fs // 512)
                        cdma = E("gpsimd", lambda e, ccx=ccx, th=th, fs=fs: e.dma_start(
                            out=yout_r[:, ccx, th * 512:(th + 1) * 512],
                            in_=tf32[:, fs:fs + 512], accum_op=ALU.add),
                            wait=[("sAC", cev), ("xinit", c_init)], inc=(af, 16))
                        tf_free[fs] = [(af, cdma)]
                        st.setdefault("chunk_done", {}).setdefault(ccx, []).append((af, cdma))
            st["ln_src_wait"] = [("acc%d" % i, cnt.get("acc%d" % i, 0)) for i in range(4)] + \
                [("sPE", cnt["sPE"])]
            st["a2_free"] = [("sPE", cnt["sPE"])]

        xT_r = xT.rearrange("(cc p) t -> p cc t", p=128)

        def mk_chunk_wait():
            cd = st.pop("chunk_done", None)
            pe_snap = st.pop("xb_free_pe", None) or cnt["sPE"]
            if cd is None:
                return None
            return lambda ccx: list(cd[ccx]) + [("sPE", pe_snap)]

        if "ffn1" in phases:
            emit_ln(src_r=xT_r, chunk_wait=lambda ccx: [("sDV", c_cols)])
            emit_ffn()
        if "attn" in phases:
            emit_ln(chunk_wait=mk_chunk_wait())
            emit_qkv()
            emit_attn()
            emit_outproj()
        if "ffn2" in phases:
            emit_ln(chunk_wait=mk_chunk_wait())
            emit_ffn()

        if dbg:
            E("sync", lambda e: e.nop(), wait=[("sPE", cnt["sPE"]),
                                                     ("sDV", cnt["sDV"]), ("sAC", cnt["sAC"])])
            for nm, (dst, src) in dbg.items():
                E("sync", lambda e, dst=dst, src=src: e.dma_start(out=dst, in_=src), inc=("dbgf", 16))
        E("gpsimd", lambda e: e.nop(),
          wait=[(f, v) for f, v in cnt.items() if f not in ("sPE", "sDV", "sAC", "sGP") and v > 0])

        for k in cnt:
            assert maxwait[k] <= cnt[k], (k, maxwait[k], cnt[k])

        @block.tensor
        def _(t):
            for f in sched["tensor"]:
                f(t)

        @block.vector
        def _(v):
            for f in sched["vector"]:
                f(v)

        @block.scalar
        def _(s):
            for f in sched["scalar"]:
                f(s)

        @block.gpsimd
        def _(g):
            for f in sched["gpsimd"]:
                f(g)

        @block.sync
        def _(s):
            for f in sched["sync"]:
                f(s)

    return nc


def make_inputs(L0_attn_x, L0_attn_Wq, L0_attn_bq, L0_attn_Wk, L0_attn_bk,
                L0_attn_Wv, L0_attn_bv, L0_attn_Wo, L0_attn_bo,
                L0_ffn_gate_w, L0_ffn_up_w, L0_ffn_down_w,
                L0_ffn_gate_s, L0_ffn_up_s, L0_ffn_down_s):
    f32 = np.float32
    gwT = np.ascontiguousarray(np.asarray(L0_ffn_gate_w, f32).T)
    uwT = np.ascontiguousarray(np.asarray(L0_ffn_up_w, f32).T)
    dwT = np.ascontiguousarray(np.asarray(L0_ffn_down_w, f32).T)
    woA = np.ascontiguousarray(np.asarray(L0_attn_Wo, f32))
    tri = np.where(np.arange(128)[None, :] <= np.arange(128)[:, None], 0.0, -1e30).astype(f32)
    ident = np.eye(128, dtype=f32)
    Wq = np.asarray(L0_attn_Wq, f32)
    Wk = np.asarray(L0_attn_Wk, f32)
    Wv = np.asarray(L0_attn_Wv, f32)
    bq = np.asarray(L0_attn_bq, f32)
    bk = np.asarray(L0_attn_bk, f32)
    bv = np.asarray(L0_attn_bv, f32)
    x = np.asarray(L0_attn_x, f32)
    gate_s = np.asarray(L0_ffn_gate_s, f32)
    up_s = np.asarray(L0_ffn_up_s, f32)
    down_s = np.asarray(L0_ffn_down_s, f32)
    bo = np.asarray(L0_attn_bo, f32)

    in_maps = []
    for c in range(NCORES):
        b, r = c // 4, c % 4
        hsel = slice(4 * r, 4 * r + 4)
        cols = np.concatenate([
            gate_s.reshape(44, 128), up_s.reshape(44, 128),
            down_s.reshape(16, 128), bo.reshape(16, 128),
            bq[hsel].reshape(4, 128), bk[hsel].reshape(4, 128), bv[hsel].reshape(4, 128),
        ], axis=0)
        in_maps.append(dict(
            xT=np.ascontiguousarray(x[b, r * NT:(r + 1) * NT, :].T),
            gwT=gwT, uwT=uwT, dwT=dwT,
            wq=np.ascontiguousarray(Wq[hsel].transpose(1, 0, 2).reshape(D, 4 * HD)),
            wk=np.ascontiguousarray(Wk[hsel].transpose(1, 0, 2).reshape(D, 4 * HD)),
            wv=np.ascontiguousarray(Wv[hsel].transpose(1, 0, 2).reshape(D, 4 * HD)),
            wo=woA,
            cols=np.ascontiguousarray(cols.T),
            trimask=tri, identf=ident,
        ))
    return in_maps


def kernel(**inputs):
    in_maps = make_inputs(**inputs)
    nc = build_nc()
    res = run_bass_kernel_spmd(nc, in_maps, core_ids=list(range(NCORES)))
    out = np.empty((B, T, D), np.float32)
    for c in range(NCORES):
        b, r = c // 4, c % 4
        out[b, r * NT:(r + 1) * NT, :] = np.asarray(res.results[c]["yout"]).T
    return out


# version 12 (final)
# speedup vs baseline: 1.0749x; 1.0215x over previous
import numpy as np
from contextlib import ExitStack
import concourse.bass as bass
import concourse.mybir as mybir
from concourse.bass_utils import run_bass_kernel_spmd

F32 = mybir.dt.float32
BF16 = mybir.dt.bfloat16
AF = mybir.ActivationFunctionType
ALU = mybir.AluOpType
AX = mybir.AxisListType

B, T, D, H, HD, INTER = 2, 4096, 2048, 16, 128, 5632
NCORES, GRP = 8, 4
NT = 1024
CC = D // 128
ICH = INTER // 128
IGRP = 2
ICG = ICH // IGRP
HPC = 4
NG = 8
QT = 4
SC = float(1.0 / np.sqrt(128.0))

RG = [[0, 1, 2, 3], [4, 5, 6, 7]]

COL_GS = 0
COL_US = 44
COL_DS = 88
COL_BO = 104
COL_BQ = 120
COL_BK = 124
COL_BV = 128
NCOLS = 132


def build_nc(phases=("ffn1", "attn", "ffn2")):
    nc = bass.Bass("TRN2", target_bir_lowering=False, num_devices=8)

    xT = nc.dram_tensor("xT", [D, NT], F32, kind="ExternalInput").ap()
    gwT = nc.dram_tensor("gwT", [D, INTER], F32, kind="ExternalInput").ap()
    uwT = nc.dram_tensor("uwT", [D, INTER], F32, kind="ExternalInput").ap()
    dwT = nc.dram_tensor("dwT", [INTER, D], F32, kind="ExternalInput").ap()
    wq = nc.dram_tensor("wq", [D, HPC * HD], F32, kind="ExternalInput").ap()
    wk = nc.dram_tensor("wk", [D, HPC * HD], F32, kind="ExternalInput").ap()
    wv = nc.dram_tensor("wv", [D, HPC * HD], F32, kind="ExternalInput").ap()
    wo = nc.dram_tensor("wo", [D, D], F32, kind="ExternalInput").ap()
    colsD = nc.dram_tensor("cols", [128, NCOLS], F32, kind="ExternalInput").ap()
    triD = nc.dram_tensor("trimask", [128, 128], F32, kind="ExternalInput").ap()
    identD = nc.dram_tensor("identf", [128, 128], F32, kind="ExternalInput").ap()
    yout = nc.dram_tensor("yout", [D, NT], F32, kind="ExternalOutput").ap()

    h2src = nc.dram_tensor("h2src", [D, NT], BF16, kind="Internal").ap()
    h2all = nc.dram_tensor("h2all", [GRP * D, NT], BF16, kind="Internal").ap()
    qD = nc.dram_tensor("qD", [HPC * HD, T], BF16, kind="Internal").ap()
    kD = nc.dram_tensor("kD", [HPC * HD, T], BF16, kind="Internal").ap()
    vD = nc.dram_tensor("vD", [HPC * T, HD], BF16, kind="Internal").ap()
    ysrc = nc.dram_tensor("ysrc", [HPC * HD, T], BF16, kind="Internal").ap()
    yallD = nc.dram_tensor("yallD", [D, T], BF16, kind="Internal").ap()
    dbg = {}
    dbg2 = {}
    if "dbg2" in phases:
        for nm, shp, dt in (("dP", [128, 16384], BF16), ("dnm", [128, 32], F32),
                            ("dsm", [128, 32], F32), ("dgr", [128, 32], F32),
                            ("dsc", [128, 4], F32), ("dysb", [128, 4096], BF16),
                            ("dpt", [128, 4096], BF16)):
            dbg2[nm] = nc.dram_tensor(nm, shp, dt, kind="ExternalOutput").ap()
    if "dbg" in phases:
        for nm, src in (("dq", qD), ("dk", kD), ("dv", vD), ("dy", ysrc), ("dyall", yallD), ("dh2", h2all)):
            dbg[nm] = (nc.dram_tensor(nm, list(src.shape), BF16, kind="ExternalOutput").ap(), src)

    gwT_r = gwT.rearrange("(cc p) i -> p cc i", p=128)
    uwT_r = uwT.rearrange("(cc p) i -> p cc i", p=128)
    dwT_r = dwT.rearrange("(ic p) h -> p ic h", p=128)
    wq_r = wq.rearrange("(cc p) d -> p cc d", p=128)
    wk_r = wk.rearrange("(cc p) d -> p cc d", p=128)
    wv_r = wv.rearrange("(cc p) d -> p cc d", p=128)
    wo_r = wo.rearrange("(hc p) c -> p hc c", p=128)
    yout_r = yout.rearrange("(cc p) t -> p cc t", p=128)
    h2src_r = h2src.rearrange("(cc p) t -> p cc t", p=128)
    h2all_r = h2all.rearrange("(r cc p) t -> p r cc t", p=128, cc=CC)
    qD_r = qD.rearrange("(h p) t -> p h t", p=128)
    kD_r = kD.rearrange("(h p) t -> p h t", p=128)
    vD_r = vD.rearrange("(h tb p) d -> p h tb d", p=128, tb=T // 128)
    ysrc_r = ysrc.rearrange("(h p) t -> p h t", p=128)
    yallD_r = yallD.rearrange("(hc p) t -> p hc t", p=128)

    with ExitStack() as _ctx:
        A1 = _ctx.enter_context(nc.sbuf_tensor("A1", [128, 16384], BF16))
        A2 = _ctx.enter_context(nc.sbuf_tensor("A2", [128, 24576], BF16))
        A3 = _ctx.enter_context(nc.sbuf_tensor("A3", [128, 16384], BF16))
        A4 = _ctx.enter_context(nc.sbuf_tensor("A4", [128, 24576], BF16))
        sqb = _ctx.enter_context(nc.sbuf_tensor("sqb", [128, 2048], BF16))
        tta = _ctx.enter_context(nc.sbuf_tensor("tta", [128, 2048], BF16))
        tvf = _ctx.enter_context(nc.sbuf_tensor("tvf", [128, 1024], BF16))
        ptsb = _ctx.enter_context(nc.sbuf_tensor("ptsb", [128, 4096], BF16))
        tf32 = _ctx.enter_context(nc.sbuf_tensor("tf32", [128, 2048], F32))
        colsf = _ctx.enter_context(nc.sbuf_tensor("colsf", [128, NCOLS], F32))
        trif = _ctx.enter_context(nc.sbuf_tensor("trif", [128, 128], F32))
        identb = _ctx.enter_context(nc.sbuf_tensor("identb", [128, 128], BF16))
        trib = _ctx.enter_context(nc.sbuf_tensor("trib", [128, 128], BF16))
        onesb = _ctx.enter_context(nc.sbuf_tensor("onesb", [128, 1], BF16))
        onesrow = _ctx.enter_context(nc.sbuf_tensor("onesrow", [1, 128], F32))
        epsc = _ctx.enter_context(nc.sbuf_tensor("epsc", [128, 1], F32))
        murow = _ctx.enter_context(nc.sbuf_tensor("murow", [1, 512], F32))
        varrow = _ctx.enter_context(nc.sbuf_tensor("varrow", [1, 512], F32))
        rrow = _ctx.enter_context(nc.sbuf_tensor("rrow", [1, 512], F32))
        murrow = _ctx.enter_context(nc.sbuf_tensor("murrow", [1, 512], F32))
        s1row = _ctx.enter_context(nc.sbuf_tensor("s1row", [1, 512], F32))
        nmrow = _ctx.enter_context(nc.sbuf_tensor("nmrow", [128, 64], F32))
        smrow = _ctx.enter_context(nc.sbuf_tensor("smrow", [128, 64], F32))
        frow = _ctx.enter_context(nc.sbuf_tensor("frow", [128, 64], F32))
        growc = _ctx.enter_context(nc.sbuf_tensor("growc", [128, 64], F32))
        nminc = _ctx.enter_context(nc.sbuf_tensor("nminc", [128, 8], F32))
        scol = _ctx.enter_context(nc.sbuf_tensor("scol", [128, 8], F32))
        pb0 = _ctx.enter_context(nc.psum_tensor("pb0", [128, 512], F32))
        pb1 = _ctx.enter_context(nc.psum_tensor("pb1", [128, 512], F32))
        pb2 = _ctx.enter_context(nc.psum_tensor("pb2", [128, 512], F32))
        pbv = _ctx.enter_context(nc.psum_tensor("pbv", [128, 512], F32))
        ptrB = _ctx.enter_context(nc.psum_tensor("ptrB", [128, 2048], BF16))
        ptrA = _ctx.enter_context(nc.psum_tensor("ptrA", [128, 2048], BF16))
        sPE = _ctx.enter_context(nc.semaphore("sPE"))
        sDV = _ctx.enter_context(nc.semaphore("sDV"))
        sAC = _ctx.enter_context(nc.semaphore("sAC"))
        sGP = _ctx.enter_context(nc.semaphore("sGP"))
        block = _ctx.enter_context(nc.Block())
        pb = [pb0, pb1, pb2, pbv]

        sched = {k: [] for k in ("tensor", "vector", "scalar", "gpsimd", "sync")}
        cnt = {"sPE": 0, "sDV": 0, "sAC": 0, "sGP": 0}
        sem = {"sPE": sPE, "sDV": sDV, "sAC": sAC, "sGP": sGP}
        maxwait = dict(cnt)

        def S(name):
            if name not in sem:
                sem[name] = _ctx.enter_context(nc.semaphore(name))
                cnt[name] = 0
                maxwait[name] = 0
            return sem[name]

        hwm = {k: {} for k in sched}

        def E(eng, make, wait=(), inc=None):
            wl = []
            hw = hwm[eng]
            for sname, v in wait:
                if v > 0 and hw.get(sname, -1) < v:
                    S(sname)
                    maxwait[sname] = max(maxwait[sname], v)
                    hw[sname] = v
                    wl.append((sem[sname], v))
            ib = None
            if inc:
                sname, amt = inc
                S(sname)
                cnt[sname] += amt
                ib = (sem[sname], amt)

            def run(e, make=make, wl=wl, ib=ib):
                for s, v in wl:
                    e.wait_ge(s, v)
                ins = make(e)
                if ib is not None:
                    ins.then_inc(*ib)
            sched[eng].append(run)
            return cnt[inc[0]] if inc else None

        pend = {}

        def bank_wait(bid):
            return pend.pop(bid, [])

        def bank_read(bid, sname, val):
            pend.setdefault(bid, []).append((sname, val))

        import dataclasses as _dc

        def bc0(ap, n):
            return _dc.replace(ap, ap=[list(p) for p in ap.ap] + [[0, n]])

        ENGSEM = {"tensor": "sPE", "vector": "sDV", "scalar": "sAC", "gpsimd": "sGP"}

        def ES(eng, make, wait=()):
            sname = ENGSEM[eng]
            v = E(eng, make, wait=wait, inc=(sname, 1))
            maxwait[sname] = max(maxwait[sname], v)

            def w(e, ss=sem[sname], v=v):
                e.wait_ge(ss, v)
            sched[eng].append(w)
            return v

        E("sync", lambda e: e.dma_start(out=colsf[:], in_=colsD), inc=("iniS", 16))
        E("sync", lambda e: e.dma_start(out=trif[:], in_=triD), inc=("iniS", 16))
        c_init = E("sync", lambda e: e.dma_start(out=yout, in_=xT), inc=("xinit", 16))
        E("gpsimd", lambda e: e.dma_start(out=identb[:], in_=identD), inc=("iniW", 16))
        E("gpsimd", lambda e: e.dma_start(out=trib[:], in_=triD), inc=("iniW", 16))
        E("gpsimd", lambda e: e.memset(onesb[:], 1.0), inc=("sGP", 1))
        E("gpsimd", lambda e: e.memset(onesrow[:], 1.0), inc=("sGP", 1))
        c_memset = E("gpsimd", lambda e: e.memset(epsc[:], 1e-5), inc=("sGP", 1))
        E("vector", lambda e: e.tensor_scalar_mul(colsf[:, COL_BQ:COL_BQ + 4],
                                                  colsf[:, COL_BQ:COL_BQ + 4], SC),
          wait=[("iniS", 32)])
        c_cols = E("vector", lambda e: e.tensor_scalar_mul(colsf[:, COL_DS:COL_DS + 16],
                                                           colsf[:, COL_DS:COL_DS + 16], 0.5),
                   inc=("sDV", 1))

        st = {
            "ln_src_wait": [("xinit", c_init), ("sDV", c_cols)],
            "xn_ready": 0,
            "a2_free": [],
            "a4_free": [("iniS", 32), ("iniW", 32)],
            "p_free": 0,
        }
        tvf_free = {}
        ptsb_free = {}
        tf_free = {}

        def emit_ln(src_r=None, chunk_wait=None):
            xb = A1
            src_r_ = yout_r if src_r is None else src_r
            cWs = {}
            stat_pe = {}

            def emit_cast(ccx):
                w = list(st["ln_src_wait"]) if chunk_wait is None else chunk_wait(ccx)
                if ccx >= 4:
                    w = w + [("sPE", stat_pe[ccx - 4])]
                cWs[ccx] = ("lnc%d" % (ccx % 4), E("gpsimd", lambda e, ccx=ccx: e.dma_start(
                    out=xb[:, ccx * 1024:(ccx + 1) * 1024],
                    in_=src_r_[:, ccx, :]), wait=w, inc=("lnc%d" % (ccx % 4), 16)))

            for ccx in range(4):
                emit_cast(ccx)
            sq_done = [0, 0]
            mm_done = [0, 0]
            for ccx in range(CC):
                if ccx >= 1 and ccx + 3 < CC and (ccx + 3) >= 4:
                    emit_cast(ccx + 3)
                par = ccx % 2
                w = [cWs[ccx], ("sGP", c_memset)]
                if ccx >= 2:
                    w.append(("sPE", mm_done[par]))
                sq_done[par] = E("scalar", lambda e, ccx=ccx, par=par: e.activation(
                    out=sqb[:, par * 1024:(par + 1) * 1024],
                    in_=xb[:, ccx * 1024:(ccx + 1) * 1024], func=AF.Square),
                    wait=w, inc=("sAC", 1))
                for th in range(2):
                    w = [cWs[ccx]]
                    if ccx == 0:
                        w += [("sGP", c_memset)] + bank_wait(("pb", th))
                    E("tensor", lambda e, ccx=ccx, th=th: e.matmul(
                        pb[th][0:1, :], onesb[:],
                        xb[:, ccx * 1024 + th * 512: ccx * 1024 + th * 512 + 512],
                        start=(ccx == 0), stop=(ccx == CC - 1)), wait=w)
                for th in range(2):
                    w = [("sAC", sq_done[par])]
                    if ccx == 0:
                        w += bank_wait(("pb", 2 + th))
                    mm_done[par] = E("tensor", lambda e, ccx=ccx, th=th, par=par: e.matmul(
                        pb[2 + th][0:1, :], onesb[:],
                        sqb[:, par * 1024 + th * 512: par * 1024 + th * 512 + 512],
                        start=(ccx == 0), stop=(ccx == CC - 1)),
                        wait=w, inc=("sPE", 1))
                stat_pe[ccx] = mm_done[par]
            st["ln_stat_pe"] = stat_pe
            c_stats = mm_done[(CC - 1) % 2]
            cn = 0
            for th in range(2):
                ES("vector", lambda e, th=th: e.tensor_scalar_mul(
                    murow[0:1, :], pb[th][0:1, :], 1.0 / D),
                    wait=[("sPE", c_stats)] + ([("sPE", cnt["sPE"])] if th else []))
                ES("vector", lambda e: e.tensor_mul(
                    s1row[0:1, :], murow[0:1, :], murow[0:1, :]))
                cv = ES("vector", lambda e, th=th: e.scalar_tensor_tensor(
                    varrow[0:1, :], pb[2 + th][0:1, :], 1.0 / D, s1row[0:1, :],
                    ALU.mult, ALU.subtract))
                ES("scalar", lambda e: e.activation(
                    out=rrow[0:1, :], in_=varrow[0:1, :], func=AF.Ln, bias=epsc[0:1, :]),
                    wait=[("sDV", cv)])
                ca = ES("scalar", lambda e: e.activation(
                    out=rrow[0:1, :], in_=rrow[0:1, :], func=AF.Exp, scale=-0.5))
                cmu = ES("vector", lambda e: e.tensor_mul(
                    murrow[0:1, :], murow[0:1, :], rrow[0:1, :]),
                    wait=[("sAC", ca)])
                E("tensor", lambda e, th=th: e.matmul(
                    pb[th][:], onesrow[:], rrow[0:1, :], start=True, stop=True),
                    wait=[("sDV", cmu)])
                cbc = E("tensor", lambda e, th=th: e.matmul(
                    pb[2 + th][:], onesrow[:], murrow[0:1, :], start=True, stop=True),
                    inc=("sPE", 1))
                for ccx in range(CC):
                    s0 = ccx * 1024 + th * 512
                    ES("vector", lambda e, s0=s0, th=th: e.tensor_tensor(
                        xb[:, s0:s0 + 512], xb[:, s0:s0 + 512], pb[th][:], ALU.mult),
                        wait=[("sPE", cbc)] if ccx == 0 else [])
                    cn = E("vector", lambda e, s0=s0, th=th: e.tensor_tensor(
                        xb[:, s0:s0 + 512], xb[:, s0:s0 + 512], pb[2 + th][:], ALU.subtract),
                        inc=("sDV", 1))
                bank_read(("pb", th), "sDV", cn)
                bank_read(("pb", 2 + th), "sDV", cn)
            st["xn_ready"] = cn

        def emit_ffn():
            xb = A1
            hdn = A2
            gsl = [A4[:, 0:4096], A4[:, 4096:8192]]
            usl = [A4[:, 8192:12288], A4[:, 12288:16384]]
            dsl = [A4[:, 16384:16384 + 2816], A4[:, 16384 + 2816:16384 + 5632]]
            rot = [0]
            ttx = [0]
            last_hdn = 0
            for grp in range(IGRP):
                slab = {}
                for j in range(11):
                    if j == 0:
                        c0 = grp * (ICG * 128)
                        w = []
                        if grp == 0:
                            w += st["a4_free"]
                        else:
                            w.append(("sPE", cnt["sPE"]))
                        slab[(0, "g")] = ("gsA", E("gpsimd", lambda e, c0=c0: e.dma_start(
                            out=gsl[0].rearrange("p (cc i) -> p cc i", i=256),
                            in_=gwT_r[:, :, c0:c0 + 256]), wait=w, inc=("gsA", 16)))
                        slab[(0, "u")] = ("usA", E("gpsimd", lambda e, c0=c0: e.dma_start(
                            out=usl[0].rearrange("p (cc i) -> p cc i", i=256),
                            in_=uwT_r[:, :, c0:c0 + 256]), inc=("usA", 16)))
                    if j + 1 < 11:
                        c1 = grp * (ICG * 128) + (j + 1) * 256
                        b1 = (j + 1) % 2
                        w = [("sPE", slab.get((j - 1, "pe"), 0))]
                        gf = "gsA" if b1 == 0 else "gsB"
                        uf = "usA" if b1 == 0 else "usB"
                        slab[(j + 1, "g")] = (gf, E("gpsimd", lambda e, c1=c1, b1=b1: e.dma_start(
                            out=gsl[b1].rearrange("p (cc i) -> p cc i", i=256),
                            in_=gwT_r[:, :, c1:c1 + 256]), wait=w, inc=(gf, 16)))
                        slab[(j + 1, "u")] = (uf, E("gpsimd", lambda e, c1=c1, b1=b1: e.dma_start(
                            out=usl[b1].rearrange("p (cc i) -> p cc i", i=256),
                            in_=uwT_r[:, :, c1:c1 + 256]), inc=(uf, 16)))
                    bsl = j % 2
                    for icl in range(2):
                        icg = j * 2 + icl
                        ica = grp * ICG + icg
                        for th in range(2):
                            bg = rot[0] % 4
                            bu = (rot[0] + 1) % 4
                            rot[0] += 2
                            wpe = [slab[(j, "g")], slab[(j, "u")], ("sDV", st["xn_ready"])]
                            wpe += bank_wait(("pb", bg)) + bank_wait(("pb", bu))
                            for k in range(CC):
                                E("tensor", lambda e, k=k, icl=icl, th=th, bsl=bsl, bg=bg: e.matmul(
                                    pb[bg][:],
                                    gsl[bsl][:, k * 256 + icl * 128: k * 256 + icl * 128 + 128],
                                    xb[:, k * 1024 + th * 512: k * 1024 + th * 512 + 512],
                                    start=(k == 0), stop=(k == CC - 1)),
                                    wait=(wpe if k == 0 else []))
                            cpe = 0
                            for k in range(CC):
                                cpe = E("tensor", lambda e, k=k, icl=icl, th=th, bsl=bsl, bu=bu: e.matmul(
                                    pb[bu][:],
                                    usl[bsl][:, k * 256 + icl * 128: k * 256 + icl * 128 + 128],
                                    xb[:, k * 1024 + th * 512: k * 1024 + th * 512 + 512],
                                    start=(k == 0), stop=(k == CC - 1)),
                                    inc=("sPE", 1) if k == CC - 1 else None) or cpe
                            slab[(j, "pe")] = cpe
                            st["xb_free_pe"] = cpe
                            st["gu_done_pe"] = cpe
                            t0 = (ttx[0] % 2) * 1024
                            ttx[0] += 1
                            wtt = tvf_free.pop(("tta", t0), [])
                            ca1 = E("scalar", lambda e, bg=bg, ica=ica, t0=t0: e.activation(
                                out=tta[:, t0:t0 + 512], in_=pb[bg][:], func=AF.Silu,
                                scale=colsf[:, COL_GS + ica:COL_GS + ica + 1]),
                                wait=[("sPE", cpe)] + wtt, inc=("sAC", 1))
                            ca2 = E("scalar", lambda e, bu=bu, ica=ica, t0=t0: e.activation(
                                out=tta[:, t0 + 512:t0 + 1024], in_=pb[bu][:], func=AF.Identity,
                                scale=colsf[:, COL_US + ica:COL_US + ica + 1]),
                                inc=("sAC", 1))
                            bank_read(("pb", bg), "sAC", ca1)
                            bank_read(("pb", bu), "sAC", ca2)
                            hs = icg * 1024 + th * 512
                            last_hdn = E("vector", lambda e, hs=hs, t0=t0: e.tensor_mul(
                                hdn[:, hs:hs + 512], tta[:, t0:t0 + 512], tta[:, t0 + 512:t0 + 1024]),
                                wait=[("sAC", ca2)], inc=("sDV", 1))
                            tvf_free[("tta", t0)] = [("sDV", last_hdn)]
                dd = {}
                for hc in range(16):
                    if hc == 0:
                        w = [("sPE", cnt["sPE"])] if grp > 0 else []
                        dd[0] = ("dsA", E("gpsimd", lambda e, grp=grp: e.dma_start(
                            out=dsl[0].rearrange("p (ic h) -> p ic h", h=128),
                            in_=dwT_r[:, grp * ICG:(grp + 1) * ICG, 0:128]),
                            wait=w, inc=("dsA", 16)))
                    if hc + 1 < 16:
                        b1 = (hc + 1) % 2
                        df = "dsA" if b1 == 0 else "dsB"
                        dd[hc + 1] = (df, E("gpsimd", lambda e, grp=grp, hc=hc, b1=b1: e.dma_start(
                            out=dsl[b1].rearrange("p (ic h) -> p ic h", h=128),
                            in_=dwT_r[:, grp * ICG:(grp + 1) * ICG, (hc + 1) * 128:(hc + 2) * 128]),
                            wait=[("sPE", dd.get(("pe", hc - 1), 0))], inc=(df, 16)))
                    bsl = hc % 2
                    for th in range(2):
                        bk = rot[0] % 4
                        rot[0] += 1
                        wpe = [dd[hc], ("sDV", last_hdn)] + bank_wait(("pb", bk))
                        cpe = 0
                        for k in range(ICG):
                            cpe = E("tensor", lambda e, k=k, th=th, bsl=bsl, bk=bk: e.matmul(
                                pb[bk][:],
                                dsl[bsl][:, k * 128:k * 128 + 128],
                                hdn[:, k * 1024 + th * 512: k * 1024 + th * 512 + 512],
                                start=(k == 0), stop=(k == ICG - 1)),
                                wait=(wpe if k == 0 else []),
                                inc=("sPE", 1) if k == ICG - 1 else None) or cpe
                        dd[("pe", hc)] = cpe
                        fs = (rot[0] % 4) * 512
                        wf = tf_free.pop(fs, [])
                        cev = E("scalar", lambda e, bk=bk, hc=hc, fs=fs: e.activation(
                            out=tf32[:, fs:fs + 512], in_=pb[bk][:], func=AF.Identity,
                            scale=colsf[:, COL_DS + hc:COL_DS + hc + 1]),
                            wait=[("sPE", cpe)] + wf, inc=("sAC", 1))
                        bank_read(("pb", bk), "sAC", cev)
                        af = "acc%d" % (fs // 512)
                        cdma = E("gpsimd", lambda e, hc=hc, th=th, fs=fs: e.dma_start(
                            out=yout_r[:, hc, th * 512:(th + 1) * 512],
                            in_=tf32[:, fs:fs + 512], accum_op=ALU.add),
                            wait=[("sAC", cev), ("xinit", c_init)], inc=(af, 16))
                        tf_free[fs] = [(af, cdma)]
                        st.setdefault("chunk_done", {}).setdefault(hc, []).append((af, cdma))
            st["ln_src_wait"] = [("acc%d" % i, cnt.get("acc%d" % i, 0)) for i in range(4)] + \
                [("sPE", cnt["sPE"])]
            st["a2_free"] = [("sPE", cnt["sPE"])]
            st["a4_free"] = [("sPE", cnt["sPE"])]

        def emit_qkv():
            xb = A1
            cS = E("sync", lambda e: e.dma_start(
                out=h2src_r, in_=xb[:].rearrange("p (cc t) -> p cc t", cc=CC)),
                wait=[("sDV", st["xn_ready"])], inc=("h2s", 16))
            ccc = E("gpsimd", lambda e: e.collective_compute(
                "AllGather", ALU.bypass, RG, ins=[h2src], outs=[h2all]),
                wait=[("h2s", cS)], inc=("cc1", 1))
            wqb = A4[:, 0:8192]
            wkb = A4[:, 8192:16384]
            wvb = A4[:, 16384:24576]
            E("gpsimd", lambda e: e.dma_start(
                out=wqb.rearrange("p (cc d) -> p cc d", d=512), in_=wq_r),
                wait=[("sPE", st.get("gu_done_pe", 0))], inc=("wqv", 16))
            E("gpsimd", lambda e: e.dma_start(
                out=wkb.rearrange("p (cc d) -> p cc d", d=512), in_=wk_r), inc=("wqv", 16))
            cw = E("gpsimd", lambda e: e.dma_start(
                out=wvb.rearrange("p (cc d) -> p cc d", d=512), in_=wv_r),
                wait=st["a4_free"], inc=("wqv", 16))
            h2sl = [A3[:, 0:8192], A3[:, 8192:16384]]
            slab = {}
            rot = [0]
            tr = [0]
            for s in range(8):
                bsl = s % 2
                wsl = [("cc1", ccc)] if s < 2 else [("sPE", slab[("pe", s - 2)])]
                hf = "h2A" if bsl == 0 else "h2B"
                slab[s] = (hf, E("sync", lambda e, s=s, bsl=bsl: e.dma_start(
                    out=h2sl[bsl].rearrange("p (cc t) -> p cc t", t=512),
                    in_=h2all_r[:, s // 2, :, (s % 2) * 512:(s % 2) * 512 + 512]),
                    wait=wsl, inc=(hf, 16)))
                for hh in range(HPC):
                    for which in ("q", "k", "v"):
                        wslb = {"q": wqb, "k": wkb, "v": wvb}[which]
                        bkk = rot[0] % 4
                        rot[0] += 1
                        wpe = [slab[s], ("wqv", cw)] + bank_wait(("pb", bkk))
                        cpe = 0
                        for k in range(CC):
                            cpe = E("tensor", lambda e, k=k, hh=hh, bsl=bsl, bkk=bkk, wslb=wslb: e.matmul(
                                pb[bkk][:],
                                wslb[:, k * 512 + hh * 128: k * 512 + hh * 128 + 128],
                                h2sl[bsl][:, k * 512:(k + 1) * 512],
                                start=(k == 0), stop=(k == CC - 1)),
                                wait=(wpe if k == 0 else []),
                                inc=("sPE", 1) if k == CC - 1 else None) or cpe
                        slab[("pe", s)] = cpe
                        ts0 = (tr[0] % 2) * 512
                        tr[0] += 1
                        wst = tvf_free.pop(ts0, [])
                        if which == "q":
                            bias_col, scale, dst = COL_BQ, SC, qD_r
                        elif which == "k":
                            bias_col, scale, dst = COL_BK, 1.0, kD_r
                        else:
                            bias_col, scale, dst = COL_BV, 1.0, None
                        cev = E("scalar", lambda e, bkk=bkk, bias_col=bias_col, hh=hh, scale=scale, ts0=ts0: e.activation(
                            out=tvf[:, ts0:ts0 + 512], in_=pb[bkk][:],
                            func=AF.Identity, scale=scale,
                            bias=colsf[:, bias_col + hh:bias_col + hh + 1]),
                            wait=[("sPE", cpe)] + wst, inc=("sAC", 1))
                        bank_read(("pb", bkk), "sAC", cev)
                        if which in ("q", "k"):
                            qf = "qkA" if ts0 == 0 else "qkB"
                            cst = E("sync", lambda e, dst=dst, hh=hh, s=s, ts0=ts0: e.dma_start(
                                out=dst[:, hh, s * 512:(s + 1) * 512], in_=tvf[:, ts0:ts0 + 512]),
                                wait=[("sAC", cev)], inc=(qf, 16))
                            tvf_free[ts0] = [(qf, cst)]
                        else:
                            vs = (s % 2) * 1024
                            wtr = bank_wait(("ptr", 0)) + [("sAC", cev), ("iniW", 32)]
                            ctr = 0
                            for sub in range(4):
                                ctr = E("tensor", lambda e, sub=sub, ts0=ts0, vs=vs: e.transpose(
                                    ptrA[:, vs + sub * 128: vs + sub * 128 + 128],
                                    tvf[:, ts0 + sub * 128: ts0 + sub * 128 + 128], identb[:]),
                                    wait=(wtr if sub == 0 else []),
                                    inc=("sPE", 1) if sub == 3 else None) or ctr
                            tvf_free[ts0] = [("sPE", ctr)]
                            vs0 = (s % 2) * 2048
                            cve = E("vector", lambda e, vs=vs, vs0=vs0: e.tensor_copy(
                                ptsb[:, vs0:vs0 + 512], ptrA[:, vs:vs + 512]),
                                wait=[("sPE", ctr)] + ptsb_free.pop(vs0, []), inc=("sDV", 1))
                            bank_read(("ptr", 0), "sDV", cve)
                            vf = "vsA" if vs0 == 0 else "vsB"
                            cvs = E("sync", lambda e, hh=hh, s=s, vs0=vs0: e.dma_start(
                                out=vD_r[:, hh, s * 4:(s + 1) * 4, :],
                                in_=ptsb[:, vs0:vs0 + 512].rearrange("p (tb d) -> p tb d", d=128)),
                                wait=[("sDV", cve)], inc=(vf, 16))
                            ptsb_free[vs0] = [(vf, cvs)]
            st["qkv_done"] = [(f, cnt.get(f, 0)) for f in ("qkA", "qkB", "vsA", "vsB")]
            st["a4_free"] = [("sPE", cnt["sPE"])]

        def emit_attn():
            ysb = A1
            qh = [A2[:, 0:4096], A2[:, 4096:8192]]
            kh = [A2[:, 8192:12288], A2[:, 12288:16384]]
            vh = [A2[:, 16384:20480], A2[:, 20480:24576]]
            Pb = [A3, A4[:, 0:16384]]
            yb = [pbv, pbv]
            ptrX = [ptrA, ptrB]
            hload = {}

            def load_head(hh, extra=()):
                bb = hh % 2
                hf = "hlA" if bb == 0 else "hlB"
                w = list(st["qkv_done"]) + list(extra)
                if hh < 2:
                    w += st["a2_free"]
                E("sync", lambda e, hh=hh, bb=bb: e.dma_start(out=qh[bb][:], in_=qD_r[:, hh, :]),
                  wait=w, inc=(hf, 16))
                E("sync", lambda e, hh=hh, bb=bb: e.dma_start(out=kh[bb][:], in_=kD_r[:, hh, :]),
                  inc=(hf, 16))
                hload[hh] = (hf, E("sync", lambda e, hh=hh, bb=bb: e.dma_start(
                    out=vh[bb][:].rearrange("p (tb d) -> p tb d", d=128), in_=vD_r[:, hh, :, :]),
                    inc=(hf, 16)))

            load_head(0)
            load_head(1)
            srot = [0]
            pfree = {0: 0, 1: 0}

            def gen_scores_softmax(hh, g, u, out):
                bb = hh % 2
                nkb = g + 1
                P = Pb[u]
                ro = u * 32
                co = u * 4
                crs_gp = 0
                czz_u = [cnt["sGP"]]
                for qt in range(QT):
                    cex = 0
                    for kb in range(nkb):
                        wdt = 512 if kb < g else (qt + 1) * 128
                        bkk = srot[0] % 3
                        srot[0] += 1
                        slot = ro + qt * 8 + kb
                        wpe = [hload[hh]] + bank_wait(("pb", bkk))
                        diag = (kb == g)
                        cpe0 = E("tensor", lambda e, qt=qt, kb=kb, wdt=wdt, bkk=bkk, bb=bb, g=g, diag=diag: e.matmul(
                            pb[bkk][:, 0:wdt],
                            qh[bb][:, g * 512 + qt * 128: g * 512 + qt * 128 + 128],
                            kh[bb][:, kb * 512: kb * 512 + wdt],
                            start=True, stop=not diag),
                            wait=wpe, inc=(None if diag else ("sPE", 1)))
                        if diag:
                            cpe = E("tensor", lambda e, qt=qt, bkk=bkk, wdt=wdt: e.matmul(
                                pb[bkk][:, qt * 128:(qt + 1) * 128],
                                identb[:], trib[:],
                                start=False, stop=True),
                                inc=("sPE", 1))
                        else:
                            cpe = cpe0
                        cmx = E("vector", lambda e, bkk=bkk, slot=slot, wdt=wdt: e.reduce_max(
                            nmrow[:, slot:slot + 1], pb[bkk][:, 0:wdt],
                            axis=AX.X, negate=True),
                            wait=[("sPE", cpe)], inc=("sDV", 1))
                        cex = E("scalar", lambda e, bkk=bkk, qt=qt, kb=kb, wdt=wdt, slot=slot, P=P: e.activation(
                            out=P[:, qt * 4096 + kb * 512: qt * 4096 + kb * 512 + wdt],
                            in_=pb[bkk][:, 0:wdt], func=AF.Exp,
                            bias=nmrow[:, slot:slot + 1],
                            accum_out=smrow[:, slot:slot + 1]),
                            wait=[("sDV", cmx), ("sPE", pfree[u])], inc=("sAC", 1))
                        bank_read(("pb", bkk), "sAC", cex)
                        yield
                    if qt < 3:
                        E("gpsimd", lambda e, qt=qt, g=g, P=P: e.memset(
                            P[:, qt * 4096 + g * 512 + (qt + 1) * 128: qt * 4096 + (g + 1) * 512],
                            0.0), wait=[("sAC", cex), ("sPE", pfree[u])], inc=("sGP", 1))
                    czz_u[0] = cnt["sGP"]
                def nm3(t, w):
                    return t[:, ro:ro + 32].rearrange("p (q k) -> p q k", k=8)[:, :, 0:w]
                ES("vector", lambda e, nkb=nkb, ro=ro, co=co: e.tensor_reduce(
                    nminc[:, co:co + 4], nm3(nmrow, nkb), AX.X, ALU.min),
                    wait=[("sAC", cex)])
                ES("vector", lambda e, nkb=nkb, ro=ro, co=co: e.tensor_tensor(
                    nm3(frow, nkb), nm3(nmrow, nkb), bc0(nminc[:, co:co + 4], nkb),
                    ALU.subtract))
                cfr = ES("scalar", lambda e, nkb=nkb, ro=ro: e.activation(
                    out=nm3(frow, nkb), in_=nm3(frow, nkb), func=AF.Exp, scale=-1.0),
                    wait=[("sDV", cnt["sDV"])])
                ES("vector", lambda e, nkb=nkb, ro=ro: e.tensor_mul(
                    nm3(growc, nkb), nm3(frow, nkb), nm3(smrow, nkb)), wait=[("sAC", cfr)])
                ES("vector", lambda e, nkb=nkb, ro=ro, co=co: e.reduce_sum(
                    scol[:, co:co + 4], nm3(growc, nkb), AX.X))
                ES("vector", lambda e, co=co: e.reciprocal(
                    scol[:, co:co + 4], scol[:, co:co + 4]))
                cgr = ES("vector", lambda e, nkb=nkb, ro=ro, co=co: e.tensor_tensor(
                    nm3(growc, nkb), nm3(frow, nkb), bc0(scol[:, co:co + 4], nkb),
                    ALU.mult))
                crs_gp = cgr
                crs_ac = 0
                for qt in range(QT):
                    for kb in range(nkb):
                        wdt = 512 if kb < g else (qt + 1) * 128
                        if kb % 2 == 0:
                            crs_gp = E("vector", lambda e, qt=qt, kb=kb, wdt=wdt, ro=ro, P=P: e.tensor_scalar(
                                P[:, qt * 4096 + kb * 512: qt * 4096 + kb * 512 + wdt],
                                P[:, qt * 4096 + kb * 512: qt * 4096 + kb * 512 + wdt],
                                growc[:, ro + qt * 8 + kb:ro + qt * 8 + kb + 1], None, ALU.mult),
                                wait=([("sAC", cgr)] if False else []), inc=("sDV", 1))
                        else:
                            crs_ac = E("scalar", lambda e, qt=qt, kb=kb, wdt=wdt, ro=ro, P=P: e.activation(
                                out=P[:, qt * 4096 + kb * 512: qt * 4096 + kb * 512 + wdt],
                                in_=P[:, qt * 4096 + kb * 512: qt * 4096 + kb * 512 + wdt],
                                func=AF.Identity,
                                scale=growc[:, ro + qt * 8 + kb:ro + qt * 8 + kb + 1]),
                                wait=[("sDV", cgr)], inc=("sAC", 1))
                out[0] = ((crs_gp, crs_ac), czz_u[0])
                yield

            def gen_transp_pv(hh, g, u, crs2, czz):
                crs_gp, crs_ac = crs2
                bb = hh % 2
                nkb = g + 1
                P = Pb[u]
                ybk = yb[u]
                cpv = 0
                for kb in range(nkb):
                    pslot = kb % 2
                    ptr = ptrX[pslot]
                    wtr = bank_wait(("ptr", pslot)) + [("sDV", crs_gp), ("sAC", crs_ac), ("sGP", czz)]
                    first = True
                    ctr = 0
                    for qt in range(QT):
                        for sub in range(4):
                            last = (qt == QT - 1 and sub == 3)
                            ctr = E("tensor", lambda e, qt=qt, kb=kb, sub=sub, P=P, ptr=ptr: e.transpose(
                                ptr[:, sub * 512 + qt * 128: sub * 512 + qt * 128 + 128],
                                P[:, qt * 4096 + kb * 512 + sub * 128: qt * 4096 + kb * 512 + sub * 128 + 128],
                                identb[:]),
                                wait=(wtr if first else []),
                                inc=("sPE", 1) if last else None) or ctr
                            first = False
                    ps0 = pslot * 2048
                    pw = ptsb_free.pop(ps0, [])
                    cptA = E("vector", lambda e, ps0=ps0, ptr=ptr: e.tensor_copy(
                        ptsb[:, ps0:ps0 + 1024], ptr[:, 0:1024]),
                        wait=[("sPE", ctr)] + pw, inc=("sDV", 1))
                    cptB = E("scalar", lambda e, ps0=ps0, ptr=ptr: e.activation(
                        out=ptsb[:, ps0 + 1024:ps0 + 2048], in_=ptr[:, 1024:2048], func=AF.Identity),
                        wait=[("sPE", ctr)] + pw, inc=("sAC", 1))
                    bank_read(("ptr", pslot), "sDV", cptA)
                    bank_read(("ptr", pslot), "sAC", cptB)
                    for sub in range(4):
                        wv_ = [("sDV", cptA), ("sAC", cptB)] if sub == 0 else []
                        if kb == 0 and sub == 0:
                            wv_ += bank_wait(("pbv", 0))
                        cpv = E("tensor", lambda e, kb=kb, sub=sub, bb=bb, ps0=ps0, ybk=ybk,
                                lastmm=(kb == nkb - 1 and sub == 3): e.matmul(
                            ybk[:],
                            vh[bb][:, (kb * 4 + sub) * 128:(kb * 4 + sub) * 128 + 128],
                            ptsb[:, ps0 + sub * 512: ps0 + sub * 512 + 512],
                            start=(kb == 0 and sub == 0), stop=lastmm),
                            wait=wv_,
                            inc=("sPE", 1) if sub == 3 else None) or cpv
                    ptsb_free[ps0] = [("sPE", cpv)]
                    yield
                pfree[u] = cpv
                cye = E("scalar", lambda e, hh=hh, g=g, ybk=ybk: e.activation(
                    out=ysb[:, hh * 4096 + g * 512: hh * 4096 + g * 512 + 512],
                    in_=ybk[:], func=AF.Identity),
                    wait=[("sPE", cpv)], inc=("sAC", 1))
                bank_read(("pbv", 0), "sAC", cye)
                return cpv

            for pair in range(HPC // 2):
                h0, h1 = 2 * pair, 2 * pair + 1
                units = [(h, g) for g in range(NG) for h in (h0, h1)]
                if dbg2 and pair == 0:
                    units = units[:1]
                prev = None
                for uix, (hh, g) in enumerate(units):
                    u = uix % 2
                    box = [None]
                    gs = gen_scores_softmax(hh, g, u, box)
                    gt = list() if prev is None else [gen_transp_pv(*prev)]
                    ns = 4 * (g + 1) + 1
                    nt = (prev[1] + 1) if prev else 0
                    ti = 0
                    for si in range(ns):
                        try:
                            next(gs)
                        except StopIteration:
                            pass
                        want_t = ((si + 1) * nt) // ns if nt else 0
                        while gt and ti < want_t:
                            try:
                                next(gt[0])
                                ti += 1
                            except StopIteration:
                                gt = []
                    for _ in gs:
                        pass
                    if gt:
                        for _ in gt[0]:
                            pass
                    crs_gp, czz = box[0]
                    prev = (hh, g, u, crs_gp, czz)
                gtl = gen_transp_pv(*prev)
                for _ in gtl:
                    pass
                if dbg2:
                    break
                if 2 * pair + 2 < HPC:
                    snap = cnt["sPE"]
                    load_head(2 * pair + 2, extra=[("sPE", snap)])
                    load_head(2 * pair + 3)
            if dbg2:
                E("sync", lambda e: e.nop(), wait=[("sAC", cnt["sAC"]), ("sDV", cnt["sDV"]),
                                                   ("sPE", cnt["sPE"]), ("sGP", cnt["sGP"])])
                for nm, src in (("dP", A3), ("dnm", nmrow[:, 0:32]), ("dsm", smrow[:, 0:32]),
                                ("dgr", growc[:, 0:32]), ("dsc", scol[:, 0:4]),
                                ("dysb", ysb[:, 0:4096]), ("dpt", ptsb)):
                    E("sync", lambda e, nm=nm, src=src: e.dma_start(out=dbg2[nm], in_=src[:] if hasattr(src, "ap") else src),
                      inc=("dbgf", 16))
            cys = E("sync", lambda e: e.dma_start(
                out=ysrc_r, in_=ysb[:].rearrange("p (h t) -> p h t", h=HPC)),
                wait=[("sAC", cnt["sAC"])], inc=("ysr", 16))
            st["y_cc"] = E("gpsimd", lambda e: e.collective_compute(
                "AllGather", ALU.bypass, RG, ins=[ysrc], outs=[yallD]),
                wait=[("ysr", cys)], inc=("cc2", 1))
            st["ysb_last"] = cys
            st["a2_free"] = [("sPE", cnt["sPE"])]
            st["a4_free"] = [("sPE", cnt["sPE"])]

        def emit_outproj():
            yall = A1
            wslb = A2[:, 0:16384]
            cyl = E("sync", lambda e: e.dma_start(
                out=yall[:].rearrange("p (hc t) -> p hc t", hc=CC),
                in_=yallD_r[:, :, bass.ts(e.partition_id() % 4, 1024)]),
                wait=[("cc2", st["y_cc"]), ("ysr", st["ysb_last"])], inc=("yal", 16))
            rot = [0]
            last_cpe = 0
            for og in range(2):
                if og == 0:
                    w = [("cc2", st["y_cc"])] + st["a2_free"]
                else:
                    w = [("sPE", last_cpe)]
                cwo = E("gpsimd", lambda e, og=og: e.dma_start(
                    out=wslb.rearrange("p (hc c) -> p hc c", c=2048),
                    in_=wo_r[:, og * 8:(og + 1) * 8, :]), wait=w, inc=("wos", 16))
                for ccx in range(CC):
                    for th in range(2):
                        bkk = rot[0] % 4
                        rot[0] += 1
                        wpe = [("wos", cwo), ("yal", cyl)] + bank_wait(("pb", bkk))
                        cpe = 0
                        for k in range(8):
                            cpe = E("tensor", lambda e, k=k, og=og, ccx=ccx, th=th, bkk=bkk: e.matmul(
                                pb[bkk][:],
                                wslb[:, k * 2048 + ccx * 128: k * 2048 + ccx * 128 + 128],
                                yall[:, (og * 8 + k) * 1024 + th * 512: (og * 8 + k) * 1024 + th * 512 + 512],
                                start=(k == 0), stop=(k == 7)),
                                wait=(wpe if k == 0 else []),
                                inc=("sPE", 1) if k == 7 else None) or cpe
                        last_cpe = cpe
                        fs = (rot[0] % 4) * 512
                        wf = tf_free.pop(fs, [])
                        if og == 0:
                            cev = E("scalar", lambda e, bkk=bkk, ccx=ccx, fs=fs: e.activation(
                                out=tf32[:, fs:fs + 512], in_=pb[bkk][:], func=AF.Identity,
                                bias=colsf[:, COL_BO + ccx:COL_BO + ccx + 1]),
                                wait=[("sPE", cpe)] + wf, inc=("sAC", 1))
                        else:
                            cev = E("scalar", lambda e, bkk=bkk, fs=fs: e.activation(
                                out=tf32[:, fs:fs + 512], in_=pb[bkk][:], func=AF.Identity),
                                wait=[("sPE", cpe)] + wf, inc=("sAC", 1))
                        bank_read(("pb", bkk), "sAC", cev)
                        af = "acc%d" % (fs // 512)
                        cdma = E("gpsimd", lambda e, ccx=ccx, th=th, fs=fs: e.dma_start(
                            out=yout_r[:, ccx, th * 512:(th + 1) * 512],
                            in_=tf32[:, fs:fs + 512], accum_op=ALU.add),
                            wait=[("sAC", cev), ("xinit", c_init)], inc=(af, 16))
                        tf_free[fs] = [(af, cdma)]
                        st.setdefault("chunk_done", {}).setdefault(ccx, []).append((af, cdma))
            st["ln_src_wait"] = [("acc%d" % i, cnt.get("acc%d" % i, 0)) for i in range(4)] + \
                [("sPE", cnt["sPE"])]
            st["a2_free"] = [("sPE", cnt["sPE"])]

        xT_r = xT.rearrange("(cc p) t -> p cc t", p=128)

        def mk_chunk_wait():
            cd = st.pop("chunk_done", None)
            pe_snap = st.pop("xb_free_pe", None) or cnt["sPE"]
            if cd is None:
                return None
            return lambda ccx: list(cd[ccx]) + [("sPE", pe_snap)]

        if "ffn1" in phases:
            emit_ln(src_r=xT_r, chunk_wait=lambda ccx: [("sDV", c_cols)])
            emit_ffn()
        if "attn" in phases:
            emit_ln(chunk_wait=mk_chunk_wait())
            emit_qkv()
            emit_attn()
            emit_outproj()
        if "ffn2" in phases:
            emit_ln(chunk_wait=mk_chunk_wait())
            emit_ffn()

        if dbg:
            E("sync", lambda e: e.nop(), wait=[("sPE", cnt["sPE"]),
                                                     ("sDV", cnt["sDV"]), ("sAC", cnt["sAC"])])
            for nm, (dst, src) in dbg.items():
                E("sync", lambda e, dst=dst, src=src: e.dma_start(out=dst, in_=src), inc=("dbgf", 16))
        E("gpsimd", lambda e: e.nop(),
          wait=[(f, v) for f, v in cnt.items() if f not in ("sPE", "sDV", "sAC", "sGP") and v > 0])

        for k in cnt:
            assert maxwait[k] <= cnt[k], (k, maxwait[k], cnt[k])

        @block.tensor
        def _(t):
            for f in sched["tensor"]:
                f(t)

        @block.vector
        def _(v):
            for f in sched["vector"]:
                f(v)

        @block.scalar
        def _(s):
            for f in sched["scalar"]:
                f(s)

        @block.gpsimd
        def _(g):
            for f in sched["gpsimd"]:
                f(g)

        @block.sync
        def _(s):
            for f in sched["sync"]:
                f(s)

    return nc


def make_inputs(L0_attn_x, L0_attn_Wq, L0_attn_bq, L0_attn_Wk, L0_attn_bk,
                L0_attn_Wv, L0_attn_bv, L0_attn_Wo, L0_attn_bo,
                L0_ffn_gate_w, L0_ffn_up_w, L0_ffn_down_w,
                L0_ffn_gate_s, L0_ffn_up_s, L0_ffn_down_s):
    f32 = np.float32
    gwT = np.ascontiguousarray(np.asarray(L0_ffn_gate_w, f32).T)
    uwT = np.ascontiguousarray(np.asarray(L0_ffn_up_w, f32).T)
    dwT = np.ascontiguousarray(np.asarray(L0_ffn_down_w, f32).T)
    woA = np.ascontiguousarray(np.asarray(L0_attn_Wo, f32))
    tri = np.where(np.arange(128)[None, :] <= np.arange(128)[:, None], 0.0, -1e30).astype(f32)
    ident = np.eye(128, dtype=f32)
    Wq = np.asarray(L0_attn_Wq, f32)
    Wk = np.asarray(L0_attn_Wk, f32)
    Wv = np.asarray(L0_attn_Wv, f32)
    bq = np.asarray(L0_attn_bq, f32)
    bk = np.asarray(L0_attn_bk, f32)
    bv = np.asarray(L0_attn_bv, f32)
    x = np.asarray(L0_attn_x, f32)
    gate_s = np.asarray(L0_ffn_gate_s, f32)
    up_s = np.asarray(L0_ffn_up_s, f32)
    down_s = np.asarray(L0_ffn_down_s, f32)
    bo = np.asarray(L0_attn_bo, f32)

    in_maps = []
    for c in range(NCORES):
        b, r = c // 4, c % 4
        hsel = slice(4 * r, 4 * r + 4)
        cols = np.concatenate([
            gate_s.reshape(44, 128), up_s.reshape(44, 128),
            down_s.reshape(16, 128), bo.reshape(16, 128),
            bq[hsel].reshape(4, 128), bk[hsel].reshape(4, 128), bv[hsel].reshape(4, 128),
        ], axis=0)
        in_maps.append(dict(
            xT=np.ascontiguousarray(x[b, r * NT:(r + 1) * NT, :].T),
            gwT=gwT, uwT=uwT, dwT=dwT,
            wq=np.ascontiguousarray(Wq[hsel].transpose(1, 0, 2).reshape(D, 4 * HD)),
            wk=np.ascontiguousarray(Wk[hsel].transpose(1, 0, 2).reshape(D, 4 * HD)),
            wv=np.ascontiguousarray(Wv[hsel].transpose(1, 0, 2).reshape(D, 4 * HD)),
            wo=woA,
            cols=np.ascontiguousarray(cols.T),
            trimask=tri, identf=ident,
        ))
    return in_maps


def kernel(**inputs):
    in_maps = make_inputs(**inputs)
    nc = build_nc()
    res = run_bass_kernel_spmd(nc, in_maps, core_ids=list(range(NCORES)))
    out = np.empty((B, T, D), np.float32)
    for c in range(NCORES):
        b, r = c // 4, c % 4
        out[b, r * NT:(r + 1) * NT, :] = np.asarray(res.results[c]["yout"]).T
    return out
